# version 14
# baseline (speedup 1.0000x reference)
import numpy as np
import ml_dtypes
from contextlib import ExitStack

import concourse.bass as bass
import concourse.mybir as mybir
from concourse.bass_utils import run_bass_kernel_spmd

F32 = mybir.dt.float32
BF16 = mybir.dt.bfloat16
AF = mybir.ActivationFunctionType
ALU = mybir.AluOpType
AX = mybir.AxisListType

D = 1024
H = 16
HD = 64
HID = 2048
NCH = 8
EPS_LN = 1e-5
EPS_RMS = 1.1920929e-07
BASE_ROPE = 10000.0
M_ZERO = 256


class Prog:

    ENGINES = ("sync", "gpsimd", "tensor", "vector", "scalar")

    def __init__(self, nc):
        self.nc = nc
        self.q = {e: [] for e in self.ENGINES}
        self.sems = {}
        self.ctr = {}

    def sem(self, name, ctx):
        h = ctx.enter_context(self.nc.semaphore(name))
        self.sems[name] = h
        self.ctr[name] = 0
        return h

    def emit(self, eng, fn):
        self.q[eng].append(fn)

    def dma(self, eng, out, in_, sem, **kw):
        h = self.sems[sem]
        self.ctr[sem] += 16
        tgt = self.ctr[sem]

        def f(e, out=out, in_=in_, h=h, kw=kw):
            e.dma_start(out, in_, **kw).then_inc(h, 16)
        self.emit(eng, f)
        return tgt

    def wait(self, eng, sem, val):
        h = self.sems[sem]

        def f(e, h=h, val=val):
            e.wait_ge(h, val)
        self.emit(eng, f)

    def inc(self, eng, sem, k=1):
        h = self.sems[sem]
        self.ctr[sem] += k
        tgt = self.ctr[sem]

        def f(e, h=h, k=k):
            e.drain().then_inc(h, k)
        self.emit(eng, f)
        return tgt

    def op(self, eng, fn, sem=None, k=1):
        if sem is None:
            self.emit(eng, fn)
            return None
        h = self.sems[sem]
        self.ctr[sem] += k
        tgt = self.ctr[sem]

        def f(e, fn=fn, h=h, k=k):
            r = fn(e)
            if r is None:
                r = e.drain()
            r.then_inc(h, k)
        self.emit(eng, f)
        return tgt

    def drain(self, eng):
        self.emit(eng, lambda e: e.drain())

    def barrier(self):
        self.ctr["BAR"] += len(self.ENGINES)
        tgt = self.ctr["BAR"]
        h = self.sems["BAR"]
        for eng in self.ENGINES:
            def f(e, h=h, tgt=tgt):
                e.drain().then_inc(h, 1)
                e.wait_ge(h, tgt)
            self.emit(eng, f)

    def finalize(self, block):
        for eng in self.ENGINES:
            fns = self.q[eng]

            def run(e, fns=fns):
                for f in fns:
                    f(e)
            getattr(block, eng)(run)


class BankPool:

    def __init__(self, p, banks, eager=False):
        self.p = p
        self.banks = banks
        self.pending = [None] * len(banks)
        self.i = 0
        self.eager = eager

    def next(self):
        i = self.i % len(self.banks)
        self.i += 1
        w = self.pending[i]
        if w is not None:
            self.p.wait("tensor", w[0], w[1])
            self.pending[i] = None
        return i, self.banks[i]

    def done(self, i, sem, val):
        if self.eager:
            self.p.wait("tensor", sem, val)
        else:
            self.pending[i] = (sem, val)

    def flush(self, eng="tensor"):
        for i, w in enumerate(self.pending):
            if w is not None:
                self.p.wait(eng, w[0], w[1])
                self.pending[i] = None


def _ceil_div(a, b):
    return -(-a // b)



def rope_tables_tm(n_chunks):
    inv = 1.0 / BASE_ROPE ** (np.arange(0, HD, 2, dtype=np.float64) / HD)
    cos = np.zeros((128, n_chunks, HD), np.float32)
    sinS = np.zeros((128, n_chunks, HD), np.float32)
    for g in range(n_chunks):
        pos = (g * 128 + np.arange(128, dtype=np.float64))[:, None]
        ang = pos * inv[None, :]
        c = np.cos(ang).astype(np.float32)
        s = np.sin(ang).astype(np.float32)
        cos[:, g, 0:32] = c
        cos[:, g, 32:64] = c
        sinS[:, g, 0:32] = -s
        sinS[:, g, 32:64] = s
    return (cos.reshape(128, -1).astype(ml_dtypes.bfloat16),
            sinS.reshape(128, -1).astype(ml_dtypes.bfloat16))


def parity_masks(parity):
    k = np.arange(128)[:, None]
    qq = np.arange(128)[None, :]
    tri = (k <= qq).astype(np.float32)
    m = np.zeros((128, 2, 128), np.float32)
    if parity == 0:
        m[:, 0, :] = tri
        m[:, 1, :] = 0.0
    else:
        m[:, 0, :] = 1.0
        m[:, 1, :] = tri
    return m.astype(ml_dtypes.bfloat16)


def triu_mask():
    k = np.arange(128)[:, None]
    qq = np.arange(128)[None, :]
    return (k <= qq).astype(ml_dtypes.bfloat16)


def chunk_major(v):
    return np.ascontiguousarray(v.reshape(NCH, 128).T)



def prep_core_inputs(inp, b, parity, n_chunks):
    N = n_chunks * 128
    f32 = np.float32
    bf = ml_dtypes.bfloat16
    m = {}
    m["x"] = np.ascontiguousarray(np.asarray(inp["L0_attn_x"])[b, :N, :], dtype=f32)
    m["wq1"] = np.asarray(inp["L0_attn_Wq"], f32)
    m["wk1"] = np.asarray(inp["L0_attn_Wk"], f32)
    m["wv1"] = np.asarray(inp["L0_attn_Wv"], f32)
    m["wo1"] = np.asarray(inp["L0_attn_Wo"], f32)
    m["wg1"] = np.asarray(inp["L0_attn_Wg1"], f32)
    m["wg2"] = np.asarray(inp["L0_attn_Wg2"], f32)
    for nm, key in [("wq2", "L0_ffn_wq"), ("wk2", "L0_ffn_wk"), ("wv2", "L0_ffn_wv"),
                    ("wo2", "L0_ffn_wo"), ("gw", "L0_ffn_gate_w"),
                    ("e0u", "L0_ffn_e0_up"), ("e0d", "L0_ffn_e0_down"),
                    ("e1a", "L0_ffn_e1_w1"), ("e1b", "L0_ffn_e1_w2"),
                    ("e1d", "L0_ffn_e1_down"),
                    ("e2a", "L0_ffn_e2_l1"), ("e2b", "L0_ffn_e2_l2"),
                    ("e2c", "L0_ffn_e2_l3"), ("e2d", "L0_ffn_e2_l4"),
                    ("e3a", "L0_ffn_e3_down"), ("e3b", "L0_ffn_e3_up"),
                    ("e3c", "L0_ffn_e3_out")]:
        m[nm] = np.ascontiguousarray(np.asarray(inp[key], f32).T)
    m["lnw"] = chunk_major(np.asarray(inp["L0_attn_ln_w"], f32))
    m["lnb"] = chunk_major(np.asarray(inp["L0_attn_ln_b"], f32))
    m["n1w"] = chunk_major(np.asarray(inp["L0_ffn_n1_w"], f32))
    m["n2w"] = chunk_major(np.asarray(inp["L0_ffn_n2_w"], f32))
    m["n2wtok"] = np.tile(np.asarray(inp["L0_ffn_n2_w"], f32).reshape(1, D), (128, 1))
    m["qnw"] = np.asarray(inp["L0_ffn_qn_w"], f32).reshape(HD, 1)
    m["knw"] = np.asarray(inp["L0_ffn_kn_w"], f32).reshape(HD, 1)
    cos_tm, sin_tm = rope_tables_tm(n_chunks)
    m["cos_tm"] = cos_tm
    m["sin_tm"] = sin_tm
    m["ident"] = np.eye(128, dtype=bf)
    m["identf"] = np.eye(128, dtype=f32)
    m["dmask"] = np.ascontiguousarray(parity_masks(parity).reshape(128, 256))
    m["triu"] = np.ascontiguousarray(triu_mask())
    m["onesrow"] = np.ones((1, N), dtype=bf)
    m["onescol"] = np.ones((128, 1), dtype=bf)
    m["baseoff"] = np.array([[parity * 128 * D]], dtype=np.uint32)
    m["consts"] = np.tile(np.array(
        [EPS_LN, EPS_RMS, 1.0 / D, 1.0 / HD, 1e-8, 64.0, 0.0, 0.0],
        dtype=f32), (128, 1))
    return m


def assemble_output(core_outs, n_chunks):
    S = n_chunks // 2
    out = np.zeros((4, n_chunks * 128, D), np.float32)
    for b in range(4):
        for p_ in range(2):
            co = core_outs[b * 2 + p_].reshape(S, 128, D)
            for i in range(S):
                out[b, (2 * i + p_) * 128:(2 * i + p_ + 1) * 128, :] = co[i]
    return out


def build_program(n_chunks, dbg=(), detect_races=True, upto=4, p1sub=9):
    S = n_chunks // 2
    N = n_chunks * 128
    NSUP = n_chunks // 4
    T_OWN = S * 128
    NT = -(-T_OWN // 512)
    dbg = set(dbg)

    nc = bass.Bass("TRN2", target_bir_lowering=False,
                   detect_race_conditions=detect_races)

    di = {}
    def inp(name, shape, dt=F32):
        di[name] = nc.dram_tensor(name, list(shape), dt, kind="ExternalInput")
        return di[name]

    x_hbm = inp("x", (N, D))
    for nm in ["wq1", "wk1", "wv1", "wo1"]:
        inp(nm, (D, D))
    inp("wg1", (D, HD)); inp("wg2", (HD, D))
    for nm in ["wq2", "wk2", "wv2", "wo2"]:
        inp(nm, (D, D))
    inp("gw", (D, 4))
    inp("e0u", (D, HID)); inp("e0d", (HID, D))
    inp("e1a", (D, HID)); inp("e1b", (D, HID)); inp("e1d", (HID, D))
    inp("e2a", (D, HID)); inp("e2b", (HID, HID)); inp("e2c", (HID, HID)); inp("e2d", (HID, D))
    inp("e3a", (D, D)); inp("e3b", (D, HID)); inp("e3c", (HID, D))
    for nm in ["lnw", "lnb", "n1w", "n2w"]:
        inp(nm, (128, NCH))
    inp("n2wtok", (128, D))
    inp("qnw", (HD, 1)); inp("knw", (HD, 1))
    inp("cos_tm", (128, n_chunks * HD), BF16)
    inp("sin_tm", (128, n_chunks * HD), BF16)
    inp("ident", (128, 128), BF16)
    inp("identf", (128, 128))
    inp("dmask", (128, 256), BF16)
    inp("triu", (128, 128), BF16)
    inp("onesrow", (1, N), BF16)
    inp("onescol", (128, 1), BF16)
    inp("baseoff", (1, 1), mybir.dt.uint32)
    inp("consts", (128, 8))

    out_hbm = nc.dram_tensor("out", [T_OWN, D], F32, kind="ExternalOutput")
    y1G = nc.dram_tensor("y1G", [N, D], F32)
    hbG = nc.dram_tensor("hbG", [N, D], BF16)
    kfG = nc.dram_tensor("kfG", [H, 65, N], BF16)
    vG = nc.dram_tensor("vG", [H, N, 65], BF16)
    x2G = nc.dram_tensor("x2G", [T_OWN, D], F32)
    hbOwnG = nc.dram_tensor("hbOwnG", [T_OWN, D], BF16)
    y1OwnG = nc.dram_tensor("y1OwnG", [T_OWN, D], F32)
    dbg_t = {}
    for nm in dbg:
        shp = {"y1": [N, D], "hb": [N, D], "q2": [T_OWN, D], "kf": [H, 65, N],
               "vg": [H, N, 65], "ao": [T_OWN, D], "x2": [T_OWN, D],
               "xf": [T_OWN, D], "wts": [T_OWN, 4], "moe": [T_OWN, D]}[nm]
        dbg_t[nm] = nc.dram_tensor("dbg_" + nm, shp, F32, kind="ExternalOutput")

    ctx = ExitStack()
    with ctx:
        p = Prog(nc)
        for s_ in ["BAR", "LDW", "LW0", "LW1", "LW2", "LW3", "LW4", "LW5", "LK0", "LK1", "LX0", "LX1", "SX0", "SX1", "LDS", "LDG", "ST", "PE", "VE", "SC", "GP"]:
            p.sem(s_, ctx)

        def mksb(c, name, shape, dt):
            return c.enter_context(nc.sbuf_tensor(name, list(shape), dt))

        sb = lambda name, shape, dt: mksb(ctx, name, shape, dt)
        ps = lambda name, shape, dt=F32: ctx.enter_context(nc.psum_tensor(name, list(shape), dt))

        ident = sb("identsb", (128, 128), BF16)
        identf = sb("identfsb", (128, 128), F32)
        consts = sb("constssb", (128, 8), F32)
        cos_tm = sb("costm", (128, n_chunks * HD), BF16)
        sin_tm = sb("sintm", (128, n_chunks * HD), BF16)
        dmask = sb("dmasksb", (128, 2, 128), BF16)
        triu = sb("triusb", (128, 128), BF16)
        onescol = sb("onescolsb", (128, 1), BF16)
        vecs = {nm: sb(nm + "sb", (128, NCH), F32) for nm in ["lnw", "n1w", "n2w"]}
        lnb_bf = sb("lnbbf", (128, NCH), BF16)
        qnw = sb("qnwsb", (HD, 1), F32)
        knw = sb("knwsb", (HD, 1), F32)
        qnrow = sb("qnrowsb", (1, HD), F32)
        knrow = sb("knrowsb", (1, HD), F32)
        mqk = sb("mqksb", (1, 2), F32)
        mh_sb = sb("mhsb", (1, 1), F32)
        negmh = sb("negmhsb", (1, 1), F32)
        qnkn = sb("qnknsb", (HD, 1), F32)
        c1row = sb("c1row", (1, D), BF16)
        c2row = sb("c2row", (1, D), BF16)
        stat = {nm: sb("st_" + nm, (128, 4), F32) for nm in
                ["sum", "sumsq", "mu", "mun", "var", "r", "sd", "r3", "tmp", "tmp2"]}
        stat16 = sb("stat16", (128, H), F32)
        stat64 = sb("stat64", (128, 64), F32)
        stat16b = sb("stat16b", (128, H), F32)
        rows = sb("rowstage", (1, 2, 512), BF16)
        bf_scr = sb("bfscr", (128, D), BF16)
        bo_sb = sb("bo_sb", (1, 1), mybir.dt.uint32)

        pA = ps("pA", (128, 512), F32)
        pB = ps("pB", (128, 512), F32)
        pTR = ps("pTR", (128, 1024), BF16)
        pC = ps("pC", (128, 512), F32)
        pD = ps("pD", (128, 512), F32)
        pE2 = ps("pE2", (128, 512), F32)
        pF = ps("pF", (128, 512), F32)
        pG = ps("pG", (128, 512), F32)
        avP = [pD, pE2, pF, pG]
        trp = BankPool(p, [0, 512], eager=True)
        mmp = BankPool(p, [pA, pB])

        for t, name in [(consts, "consts"), (qnw, "qnw"), (knw, "knw")]:
            p.dma("sync", t[:], di[name][:, :], "LDS")
        p.dma("sync", qnrow[:], di["qnw"][:, :].rearrange("a b -> b a"), "LDS")
        p.dma("sync", knrow[:], di["knw"][:, :].rearrange("a b -> b a"), "LDS")
        for nm, t in vecs.items():
            p.dma("sync", t[:], di[nm][:, :], "LDS")
        p.dma("sync", ident[:], di["ident"][:, :], "LDS")
        p.dma("sync", identf[:], di["identf"][:, :], "LDS")
        p.dma("sync", cos_tm[:], di["cos_tm"][:, :], "LDS")
        p.dma("sync", sin_tm[:], di["sin_tm"][:, :], "LDS")
        p.dma("sync", dmask[:], di["dmask"][:, :].rearrange("p (a b) -> p a b", a=2), "LDS")
        p.dma("sync", triu[:], di["triu"][:, :], "LDS")
        p.dma("sync", onescol[:], di["onescol"][:, :], "LDS")
        p.dma("sync", bo_sb[:], di["baseoff"][:, :], "LDS")
        p.dma("gpsimd", lnb_bf[:], di["lnb"][:, :], "LDG")
        ld_const = p.ctr["LDS"]
        ld_constg = p.ctr["LDG"]

        bo_reg = ctx.enter_context(nc.gpsimd.register("bo_reg"))
        off_reg = ctx.enter_context(nc.gpsimd.register("off_reg"))
        p.wait("gpsimd", "LDS", ld_const)
        p.emit("gpsimd", lambda g: g.reg_load(bo_reg, bo_sb[:1, :1]))

        def gather_all_own(dstT, srcT, sem="LDG"):
            def f(g, dstT=dstT, srcT=srcT):
                g.dma_start(dstT[:, :],
                            bass.AP(srcT, bo_reg, [[256 * D, S], [D, 128], [1, D]])
                            ).then_inc(p.sems[sem], 16)
            p.ctr[sem] += 16
            p.emit("gpsimd", f)
            return p.ctr[sem]

        def load_weight(dst, name, sem="LDW"):
            src = di[name][:, :].rearrange("(c p) o -> p c o", p=128)
            return p.dma("gpsimd", dst[:], src, sem)

        def tr_batch(srcs, evac_fn, idf=False):
            if len(srcs) > 4:
                trp.flush()
                oi, off = None, 0
            else:
                oi, off = trp.next()
            def tf(t, srcs=srcs, off=off):
                for j, src in enumerate(srcs):
                    r = t.transpose(pTR[0:src.shape[-1], off + j * 128:off + j * 128 + src.shape[0]],
                                    src, identf[:] if idf else ident[:])
                return r
            pe_t = p.op("tensor", lambda t, f=tf: f(t), "PE")
            p.wait("vector", "PE", pe_t)
            ve_t = p.op("vector", lambda v, f=evac_fn, off=off: f(v, off), "VE")
            if oi is None:
                p.wait("tensor", "VE", ve_t)
            else:
                trp.done(oi, "VE", ve_t)
            return ve_t

        def transpose_tm_to_fm(src_ap_fn, dst_fm, tc, n_feat_chunks=NCH):
            for cb in range(0, n_feat_chunks, 8):
                nblk = min(8, n_feat_chunks - cb)
                def ev(v, off, cb=cb, nblk=nblk, tc=tc):
                    return v.tensor_copy(
                        dst_fm[:, cb:cb + nblk, tc * 128:(tc + 1) * 128],
                        pTR[:, off:off + nblk * 128].rearrange("p (a b) -> p a b", a=nblk))
                tr_batch([src_ap_fn(cb + j) for j in range(nblk)], ev)

        def stats_ln(x_ap, idx, eps_col, extra_hb=False):
            p.wait("scalar", "VE", p.ctr["VE"])
            p.drain("vector")
            sc_t = p.op("scalar", lambda s, x_ap=x_ap, idx=idx: s.activation(
                bf_scr[:, 0:x_ap.shape[-1]], x_ap, AF.Square,
                accum_out=stat["sumsq"][:, idx:idx + 1]), "SC")
            ve_t = p.op("vector", lambda v, x_ap=x_ap, idx=idx: v.reduce_sum(
                stat["sum"][:, idx:idx + 1], x_ap, axis=AX.X), "VE")
            p.wait("vector", "SC", sc_t)
            i = idx
            nfree = x_ap.shape[-1]

            def vchain(v, i=i, nfree=nfree):
                v.drain()
                v.tensor_scalar_mul(stat["mu"][:, i:i + 1], stat["sum"][:, i:i + 1], 1.0 / nfree)
                v.tensor_scalar_mul(stat["tmp"][:, i:i + 1], stat["sumsq"][:, i:i + 1], 1.0 / nfree)
                v.drain()
                v.tensor_tensor(stat["tmp2"][:, i:i + 1], stat["mu"][:, i:i + 1],
                                stat["mu"][:, i:i + 1], ALU.mult)
                v.tensor_scalar_mul(stat["mun"][:, i:i + 1], stat["mu"][:, i:i + 1], -1.0)
                v.drain()
                v.tensor_tensor(stat["var"][:, i:i + 1], stat["tmp"][:, i:i + 1],
                                stat["tmp2"][:, i:i + 1], ALU.subtract)
                return v.drain()
            ve_t = p.op("vector", lambda v, f=vchain: f(v), "VE")
            p.wait("scalar", "VE", ve_t)
            sc_t = p.op("scalar", lambda s, i=i, eps_col=eps_col: s.activation(
                stat["sd"][:, i:i + 1], stat["var"][:, i:i + 1], AF.Sqrt,
                bias=eps_col), "SC")
            p.wait("vector", "SC", sc_t)

            def vchain2(v, i=i, extra=extra_hb):
                v.reciprocal(stat["r"][:, i:i + 1], stat["sd"][:, i:i + 1])
                if extra:
                    v.drain()
                    v.tensor_tensor(stat["tmp"][:, i:i + 1], stat["var"][:, i:i + 1],
                                    stat["r"][:, i:i + 1], ALU.mult)
                    v.drain()
                    v.tensor_tensor(stat["tmp"][:, i:i + 1], stat["tmp"][:, i:i + 1],
                                    stat["r"][:, i:i + 1], ALU.mult)
                return v.drain()
            ve_t = p.op("vector", lambda v, f=vchain2: f(v), "VE")
            if extra_hb:
                p.wait("scalar", "VE", ve_t)
                sc_t = p.op("scalar", lambda s, i=i: s.activation(
                    stat["tmp2"][:, i:i + 1], stat["tmp"][:, i:i + 1], AF.Sqrt,
                    bias=consts[:, 1:2]), "SC")
                p.wait("vector", "SC", sc_t)

                def vchain3(v, i=i):
                    v.reciprocal(stat["tmp"][:, i:i + 1], stat["tmp2"][:, i:i + 1])
                    v.drain()
                    v.tensor_tensor(stat["r3"][:, i:i + 1], stat["tmp"][:, i:i + 1],
                                    stat["r"][:, i:i + 1], ALU.mult)
                    return v.drain()
                p.op("vector", lambda v, f=vchain3: f(v), "VE")

        def rms_heads(t_ap, rdst, eps_col_idx=1):
            p.wait("scalar", "VE", p.ctr["VE"])
            p.drain("vector")
            sc_t = p.op("scalar", lambda s, t_ap=t_ap: s.activation(
                bf_scr[:, 0:D], t_ap, AF.Square), "SC")
            p.wait("vector", "SC", sc_t)

            def vch(v, rdst=rdst):
                v.tensor_reduce(stat16b[:], bf_scr[:, 0:D].rearrange(
                    "p (h d) -> p h d", h=H), AX.X, ALU.add)
                v.drain()
                v.tensor_scalar_mul(stat16b[:], stat16b[:], 1.0 / HD)
                return v.drain()
            ve_t = p.op("vector", lambda v, f=vch: f(v), "VE")
            p.wait("scalar", "VE", ve_t)
            sc_t = p.op("scalar", lambda s: s.activation(
                stat16b[:], stat16b[:], AF.Sqrt, bias=consts[:, eps_col_idx:eps_col_idx + 1]), "SC")
            p.wait("vector", "SC", sc_t)

            def vch2(v, rdst=rdst):
                v.reciprocal(rdst[:], stat16b[:])
                return v.drain()
            p.op("vector", lambda v, f=vch2: f(v), "VE")

        def apply_heads(t_ap3, rsrc):
            def vap(v, t_ap3=t_ap3, rsrc=rsrc):
                for h in range(H):
                    v.tensor_scalar_mul(t_ap3[:, h, :], t_ap3[:, h, :], rsrc[:, h:h + 1])
                return v.drain()
            return p.op("vector", lambda v, f=vap: f(v), "VE")

        eps_ln_col = consts[:, 0:1]

        p1ctx = ExitStack()
        sb1 = lambda name, shape, dt: mksb(p1ctx, name, shape, dt)
        w1 = {nm: sb1(nm + "s", (128, NCH, D), BF16) for nm in ["wq1", "wk1", "wv1", "wo1"]}
        wg1s = sb1("wg1s", (128, NCH, HD), BF16)
        wg2s = sb1("wg2s", (HD, D), BF16)
        xw = sb1("xw", (128, 2, 4, D), F32)
        xb = sb1("xb", (128, D), BF16)
        x_fm = sb1("x_fm", (128, NCH, 512), BF16)
        q1t = sb1("q1t", (128, 4, D), BF16)
        k1t = sb1("k1t", (128, 4, D), BF16)
        v1t = sb1("v1t", (128, 4, D), BF16)
        ro_tmp = xb
        q1f = sb1("q1f", (64, H, 512), BF16)
        k1f = sb1("k1f", (64, H, 512), BF16)
        gate_fm = sb1("gate_fm", (128, NCH, 512), BF16)
        z_fm = x_fm
        z_tm = sb1("z_tm", (128, D), BF16)
        g1sb = z_tm[0:HD, 0:512]
        S_sb = sb1("S_sb", (64, 2, 512), BF16)
        S_f32 = sb1("S_f32", (64, 2, 512), F32)
        e_sb = z_tm[:, 512:768]
        y1buf = sb1("y1buf", (128, D), F32)
        hbbuf = sb1("hbbuf", (128, D), BF16)
        pE = pC
        pO = pD
        pSa = pE2
        pSb = pF
        pMini = pG

        for nm in ["wq1", "wk1", "wv1", "wo1"]:
            load_weight(w1[nm], nm)
        load_weight(wg1s, "wg1")
        p.dma("gpsimd", wg2s[:], di["wg2"][:, :], "LDW")
        ldw_p1 = p.ctr["LDW"]

        p.wait("vector", "LDS", ld_const)
        p.op("vector", lambda v: v.tensor_tensor(qnkn[:], qnw[:], knw[:], ALU.mult), "VE")
        def mhv(v):
            v.tensor_tensor(qnrow[:], qnrow[:], qnrow[:], ALU.mult)
            v.drain()
            v.reduce_max(mqk[:, 0:1], qnrow[:], axis=AX.X)
            v.drain()
            v.tensor_tensor(knrow[:], knrow[:], knrow[:], ALU.mult)
            v.drain()
            v.reduce_max(mqk[:, 1:2], knrow[:], axis=AX.X)
            v.drain()
            v.tensor_tensor(negmh[:], mqk[:, 0:1], mqk[:, 1:2], ALU.mult)
            return v.drain()
        ve_mh = p.op("vector", lambda v, f=mhv: f(v), "VE")
        p.wait("scalar", "LDS", ld_const)
        p.wait("scalar", "VE", ve_mh)
        sc_mh = p.op("scalar", lambda s: s.activation(
            mh_sb[:], negmh[:], AF.Sqrt, scale=consts[0:1, 5:6]), "SC")
        p.wait("vector", "SC", sc_mh)
        ve_mh2 = p.op("vector", lambda v: v.tensor_scalar_mul(negmh[:], mh_sb[:], -1.0), "VE")
        p.wait("tensor", "LDW", ldw_p1)
        p.wait("tensor", "LDG", ld_constg)
        for ot in range(2):
            def mmres(t, ot=ot):
                for c in range(NCH):
                    r = t.matmul(pMini[0:1, :], lnb_bf[:, c:c + 1],
                                 w1["wo1"][:, c, ot * 512:(ot + 1) * 512],
                                 start=(c == 0), stop=(c == NCH - 1), skip_group_check=True)
                return r
            pe_t = p.op("tensor", lambda t, f=mmres: f(t), "PE")
            p.wait("vector", "PE", pe_t)
            ve_t = p.op("vector", lambda v, ot=ot: v.tensor_copy(
                c2row[:, ot * 512:(ot + 1) * 512], pMini[0:1, :]), "VE")
            p.wait("tensor", "VE", ve_t)
        p.wait("vector", "LDW", ldw_p1)
        for c in range(NCH):
            p.op("vector", lambda v, c=c: v.tensor_scalar_mul(
                w1["wo1"][:, c, :], w1["wo1"][:, c, :], vecs["lnw"][:, c:c + 1]), "VE")
        p.drain("vector")
        fold_done = p.inc("vector", "VE")
        p.wait("tensor", "VE", fold_done)
        for ot in range(2):
            def mmc1(t, ot=ot):
                for c in range(NCH):
                    r = t.matmul(pMini[0:1, :], onescol[:],
                                 w1["wo1"][:, c, ot * 512:(ot + 1) * 512],
                                 start=(c == 0), stop=(c == NCH - 1), skip_group_check=True)
                return r
            pe_t = p.op("tensor", lambda t, f=mmc1: f(t), "PE")
            p.wait("vector", "PE", pe_t)
            ve_t = p.op("vector", lambda v, ot=ot: v.tensor_copy(
                c1row[:, ot * 512:(ot + 1) * 512], pMini[0:1, :]), "VE")
            p.wait("tensor", "VE", ve_t)

        if upto <= 0:
            p.wait("vector", "VE", p.ctr["VE"])
            p.op("gpsimd", lambda g: g.memset(y1buf[:], 0.0), "GP")
            p.wait("sync", "GP", p.ctr["GP"])
            for _i in range(S):
                p.dma("sync", out_hbm[_i * 128:(_i + 1) * 128, :], y1buf[:], "ST")
            p.wait("sync", "ST", p.ctr["ST"])
            with nc.Block() as block:
                p.finalize(block)
            p1ctx.close()
            return nc

        p.barrier()

        p.op("gpsimd", lambda g: g.memset(S_sb[:], 0.0), "GP")
        p.op("gpsimd", lambda g: g.memset(S_f32[:], 0.0), "GP")
        p.wait("scalar", "GP", p.ctr["GP"])
        p.wait("tensor", "GP", p.ctr["GP"])
        p.wait("vector", "GP", p.ctr["GP"])

        xsrc = x_hbm[:, :].rearrange("(s c p) d -> s p c d", p=128, c=4)
        lxs = lambda i: "LX0" if i % 2 == 0 else "LX1"
        xw_ld = {0: p.dma("sync", xw[:, 0, :, :], xsrc[0], lxs(0))}
        for G in range(NSUP):
            xb_ = G % 2
            prev_z_ve = [None]
            if G + 1 < NSUP:
                xw_ld[G + 1] = p.dma("sync", xw[:, (G + 1) % 2, :, :], xsrc[G + 1], lxs(G + 1))
            p.wait("vector", lxs(G), xw_ld[G])
            for tc in range(4):
                cast_t = p.op("vector", lambda v, tc=tc, xb_=xb_: v.tensor_copy(xb[:], xw[:, xb_, tc, :]), "VE")
                p.wait("tensor", "VE", cast_t)
                transpose_tm_to_fm(lambda c: xb[:, c * 128:(c + 1) * 128], x_fm, tc)
            trp.flush()
            for tc in range(4 if p1sub >= 2 else 0):
                for wname, dstt, act in [("wq1", q1t, AF.Silu), ("wk1", k1t, AF.Silu),
                                         ("wv1", v1t, None)]:
                    for ot in range(2):
                        oi, bank = mmp.next()
                        def mm(t, wname=wname, tc=tc, ot=ot, bank=bank):
                            for c in range(NCH):
                                r = t.matmul(bank[:], x_fm[:, c, tc * 128:(tc + 1) * 128],
                                             w1[wname][:, c, ot * 512:(ot + 1) * 512],
                                             start=(c == 0), stop=(c == NCH - 1),
                                             skip_group_check=True)
                            return r
                        pe_t = p.op("tensor", lambda t, f=mm: f(t), "PE")
                        if act is not None:
                            p.wait("scalar", "PE", pe_t)
                            ev_t = p.op("scalar", lambda s, dstt=dstt, tc=tc, ot=ot, bank=bank, act=act:
                                        s.activation(dstt[:, tc, ot * 512:(ot + 1) * 512], bank[:], act), "SC")
                            mmp.done(oi, "SC", ev_t)
                        else:
                            p.wait("vector", "PE", pe_t)
                            ev_t = p.op("vector", lambda v, dstt=dstt, tc=tc, ot=ot, bank=bank:
                                        v.tensor_copy(dstt[:, tc, ot * 512:(ot + 1) * 512], bank[:]), "VE")
                            mmp.done(oi, "VE", ev_t)
            sc_proj = p.ctr["SC"]
            if p1sub < 3:
                continue
            oi, bank = mmp.next()
            def mmg1(t, bank=bank):
                for c in range(NCH):
                    r = t.matmul(bank[0:HD, :], wg1s[:, c, :], x_fm[:, c, :],
                                 start=(c == 0), stop=(c == NCH - 1), skip_group_check=True)
                return r
            pe_t = p.op("tensor", lambda t, f=mmg1: f(t), "PE")
            p.wait("vector", "PE", pe_t)
            ve_t = p.op("vector", lambda v, bank=bank: v.tensor_copy(g1sb[:], bank[0:HD, :]), "VE")
            mmp.done(oi, "VE", ve_t)
            p.wait("tensor", "VE", ve_t)
            for oc in range(NCH):
                oi, bank = mmp.next()
                def mmg2(t, oc=oc, bank=bank):
                    return t.matmul(bank[:], wg2s[:, oc * 128:(oc + 1) * 128], g1sb[:],
                                    start=True, stop=True, skip_group_check=True)
                pe_t = p.op("tensor", lambda t, f=mmg2: f(t), "PE")
                p.wait("scalar", "PE", pe_t)
                sc_t = p.op("scalar", lambda s, oc=oc, bank=bank: s.activation(
                    gate_fm[:, oc, :], bank[:, 0:512], AF.Sigmoid), "SC")
                mmp.done(oi, "SC", sc_t)
            sc_gate = p.ctr["SC"]
            if p1sub < 4:
                continue
            p.wait("vector", "SC", sc_proj)
            for tc in range(4):
                g = G * 4 + tc
                for tt in (q1t, k1t):
                    def ropef(v, tt=tt, tc=tc, g=g):
                        src3 = tt[:, tc, :].rearrange("p (h d) -> p h d", h=H)
                        tmp3 = ro_tmp[:].rearrange("p (h d) -> p h d", h=H)
                        sin3lo = sin_tm[:, g * HD:g * HD + 32].unsqueeze(1).broadcast_to([128, H, 32])
                        sin3hi = sin_tm[:, g * HD + 32:g * HD + 64].unsqueeze(1).broadcast_to([128, H, 32])
                        cos3 = cos_tm[:, g * HD:(g + 1) * HD].unsqueeze(1).broadcast_to([128, H, HD])
                        v.tensor_tensor(tmp3[:, :, 0:32], src3[:, :, 32:64], sin3lo, ALU.mult)
                        v.tensor_tensor(tmp3[:, :, 32:64], src3[:, :, 0:32], sin3hi, ALU.mult)
                        v.drain()
                        v.tensor_tensor(src3, src3, cos3, ALU.mult)
                        v.drain()
                        v.tensor_tensor(tt[:, tc, :], tt[:, tc, :], ro_tmp[:], ALU.add)
                        return v.drain()
                    p.op("vector", lambda v, f=ropef: f(v), "VE")
            rope_done = p.inc("vector", "VE")
            p.wait("tensor", "VE", rope_done)
            if p1sub < 5:
                continue
            for tc in range(4):
                for src_t, dst_f in [(q1t, q1f), (k1t, k1f)]:
                    for hb_ in range(0, H, 8):
                        def evh(v, off, dst_f=dst_f, tc=tc, hb_=hb_):
                            return v.tensor_copy(
                                dst_f[0:64, hb_:hb_ + 8, tc * 128:(tc + 1) * 128],
                                pTR[0:64, off:off + 1024].rearrange("p (a b) -> p a b", a=8))
                        tr_batch([src_t[:, tc, (hb_ + j) * 64:(hb_ + j + 1) * 64]
                                  for j in range(8)], evh)
            if p1sub < 6:
                continue
            trp.flush()
            p.wait("vector", "SC", sc_gate)

            def attn_tc(tc):
                for pp in range(NCH):
                    ha, hb2 = 2 * pp, 2 * pp + 1
                    def mme(t, tc=tc, ha=ha, hb2=hb2):
                        t.matmul(pE[:, 0:128], k1f[0:64, ha, tc * 128:(tc + 1) * 128],
                                 q1f[0:64, ha, tc * 128:(tc + 1) * 128], start=True, stop=False,
                                 skip_group_check=True)
                        return t.matmul(pE[:, 128:256], k1f[0:64, hb2, tc * 128:(tc + 1) * 128],
                                        q1f[0:64, hb2, tc * 128:(tc + 1) * 128], start=False, stop=True,
                                        skip_group_check=True)
                    pe_t = p.op("tensor", lambda t, f=mme: f(t), "PE")
                    p.wait("vector", "PE", pe_t)
                    ve_t = p.op("vector", lambda v: v.tensor_tensor(
                        e_sb[:].rearrange("p (a b) -> p a b", a=2),
                        pE[:, 0:256].rearrange("p (a b) -> p a b", a=2),
                        triu[:].unsqueeze(1).broadcast_to([128, 2, 128]), ALU.mult), "VE")
                    p.wait("tensor", "VE", ve_t)
                    if prev_z_ve[0] is not None:
                        p.wait("tensor", "VE", prev_z_ve[0])
                        prev_z_ve[0] = None
                    def mmav(t, tc=tc, pp=pp, ha=ha, hb2=hb2):
                        t.matmul(pO[0:64, 0:128], v1t[:, tc, ha * 64:(ha + 1) * 64],
                                 e_sb[:, 0:128], start=True, stop=False, skip_group_check=True)
                        t.matmul(pO[64:128, 0:128], v1t[:, tc, hb2 * 64:(hb2 + 1) * 64],
                                 e_sb[:, 128:256], start=True, stop=False,
                                 tile_position=(0, 64), skip_group_check=True)
                        t.matmul(pO[0:64, 0:128], S_sb[0:64, 0, pp * 64:(pp + 1) * 64],
                                 q1f[0:64, ha, tc * 128:(tc + 1) * 128], start=False, stop=True,
                                 skip_group_check=True)
                        t.matmul(pO[64:128, 0:128], S_sb[0:64, 1, pp * 64:(pp + 1) * 64],
                                 q1f[0:64, hb2, tc * 128:(tc + 1) * 128], start=False, stop=True,
                                 tile_position=(0, 64), skip_group_check=True)
                        t.matmul(pSa[0:64, pp * 64:(pp + 1) * 64],
                                 k1t[:, tc, ha * 64:(ha + 1) * 64],
                                 v1t[:, tc, ha * 64:(ha + 1) * 64],
                                 start=(pp == 0), stop=(pp == 7), skip_group_check=True)
                        return t.matmul(pSb[0:64, pp * 64:(pp + 1) * 64],
                                        k1t[:, tc, hb2 * 64:(hb2 + 1) * 64],
                                        v1t[:, tc, hb2 * 64:(hb2 + 1) * 64],
                                        start=(pp == 0), stop=(pp == 7), skip_group_check=True)
                    pe_t = p.op("tensor", lambda t, f=mmav: f(t), "PE")
                    p.wait("vector", "PE", pe_t)
                    ve_t = p.op("vector", lambda v, tc=tc, pp=pp: v.tensor_tensor(
                        z_fm[:, pp, tc * 128:(tc + 1) * 128], pO[:, 0:128],
                        gate_fm[:, pp, tc * 128:(tc + 1) * 128], ALU.mult), "VE")
                    prev_z_ve[0] = ve_t
                def supd(v):
                    v.tensor_tensor(S_f32[0:64, 0, :], S_f32[0:64, 0, :], pSa[0:64, :], ALU.add)
                    v.tensor_tensor(S_f32[0:64, 1, :], S_f32[0:64, 1, :], pSb[0:64, :], ALU.add)
                    v.drain()
                    v.tensor_copy(S_sb[:], S_f32[:])
                    return v.drain()
                ve_t = p.op("vector", lambda v, f=supd: f(v), "VE")
                p.wait("tensor", "VE", ve_t)

            def tail_tc(tc):
                g = G * 4 + tc
                def evz(v, off):
                    return v.tensor_copy(z_tm[:, 0:1024], pTR[:, 0:1024])
                tr_batch([z_fm[:, j, tc * 128:(tc + 1) * 128] for j in range(NCH)], evz)
                stats_ln(z_tm[:, :], 0, eps_ln_col)
                p.wait("tensor", "VE", p.ctr["VE"])
                def mini2(t):
                    t.transpose(pMini[0:1, 0:128], stat["mun"][:, 0:1], identf[:])
                    return t.transpose(pMini[0:1, 128:256], stat["sd"][:, 0:1], identf[:])
                pe_t = p.op("tensor", lambda t, f=mini2: f(t), "PE")
                p.wait("vector", "PE", pe_t)
                ve_t = p.op("vector", lambda v, tc=tc: v.tensor_copy(
                    rows[0:1, 0:2, tc * 128:(tc + 1) * 128],
                    pMini[0:1, 0:256].rearrange("p (a b) -> p a b", a=2)), "VE")
                p.wait("tensor", "VE", ve_t)
                r_col = stat["r"][:, 0:1]
                for ot in range(2):
                    oi, bank = mmp.next()
                    def mmwo(t, tc=tc, ot=ot, bank=bank):
                        for c in range(NCH):
                            t.matmul(bank[:], z_fm[:, c, tc * 128:(tc + 1) * 128],
                                     w1["wo1"][:, c, ot * 512:(ot + 1) * 512],
                                     start=(c == 0), stop=False, skip_group_check=True)
                        t.matmul(bank[:], rows[0:1, 0, tc * 128:(tc + 1) * 128],
                                 c1row[0:1, ot * 512:(ot + 1) * 512], start=False, stop=False,
                                 skip_group_check=True)
                        return t.matmul(bank[:], rows[0:1, 1, tc * 128:(tc + 1) * 128],
                                        c2row[0:1, ot * 512:(ot + 1) * 512], start=False, stop=True,
                                        skip_group_check=True)
                    pe_t = p.op("tensor", lambda t, f=mmwo: f(t), "PE")
                    p.wait("vector", "PE", pe_t)
                    ve_t = p.op("vector", lambda v, tc=tc, ot=ot, bank=bank, r_col=r_col, xb_=xb_:
                                v.scalar_tensor_tensor(
                                    y1buf[:, ot * 512:(ot + 1) * 512], bank[:], r_col,
                                    xw[:, xb_, tc, ot * 512:(ot + 1) * 512], ALU.mult, ALU.add), "VE")
                    mmp.done(oi, "VE", ve_t)
                stats_ln(y1buf[:, :], 1, eps_ln_col, extra_hb=True)
                def vfin(v):
                    v.tensor_scalar(hbbuf[:], y1buf[:, :], stat["mun"][:, 1:2],
                                    stat["r3"][:, 1:2], ALU.add, ALU.mult)
                    v.drain()
                    v.tensor_scalar(y1buf[:], y1buf[:, :], stat["mun"][:, 1:2],
                                    stat["r"][:, 1:2], ALU.add, ALU.mult)
                    return v.drain()
                ve_t = p.op("vector", lambda v, f=vfin: f(v), "VE")
                p.wait("sync", "VE", ve_t)
                p.dma("sync", y1G[g * 128:(g + 1) * 128, :], y1buf[:], "ST")
                st2 = p.dma("sync", hbG[g * 128:(g + 1) * 128, :], hbbuf[:], "ST")
                p.wait("vector", "ST", st2)

            for tc in range(4):
                attn_tc(tc)
                if p1sub >= 7 and tc >= 1:
                    tail_tc(tc - 1)
            if p1sub >= 7:
                tail_tc(3)

        if "y1" in dbg and p1sub >= 7:
            p.wait("sync", "ST", p.ctr["ST"])
            for g in range(n_chunks):
                ld = p.dma("sync", y1buf[:], y1G[g * 128:(g + 1) * 128, :], "LDS")
                p.wait("sync", "LDS", ld)
                p.dma("sync", dbg_t["y1"][g * 128:(g + 1) * 128, :], y1buf[:], "ST")
                p.wait("sync", "ST", p.ctr["ST"])

        p.barrier()
        p.wait("sync", "ST", p.ctr["ST"])
        p.wait("gpsimd", "ST", p.ctr["ST"])
        gather_all_own(hbOwnG, hbG)
        gather_all_own(y1OwnG, y1G)
        own_gathers = p.ctr["LDG"]
        if upto <= 1:
            p.wait("gpsimd", "LDG", own_gathers)
            p.op("gpsimd", lambda g: g.memset(y1buf[:], 0.0), "GP")
            p.wait("sync", "GP", p.ctr["GP"])
            for _i in range(S):
                p.dma("sync", out_hbm[_i * 128:(_i + 1) * 128, :], y1buf[:], "ST")
            p.wait("sync", "ST", p.ctr["ST"])
            with nc.Block() as block:
                p.finalize(block)
            p1ctx.close()
            return nc
        p1ctx.close()

        p2ctx = ExitStack()
        sb2 = lambda name, shape, dt: mksb(p2ctx, name, shape, dt)
        unionA = ctx.enter_context(nc.sbuf_tensor("unionA", [128, S * D], BF16, side="right"))
        q2n = unionA[:].rearrange("p (s d) -> p s d", s=S)
        wts_t = ctx.enter_context(nc.sbuf_tensor("wtsT", [128, S, 4], F32, side="right"))
        wts = wts_t[:]
        attnctx = ExitStack()
        attnout_t = attnctx.enter_context(nc.sbuf_tensor("attnoutT", [128, S * D], BF16, side="right"))
        attnout = attnout_t[:].rearrange("p (s d) -> p s d", s=S)
        w2 = {nm: sb2(nm + "s", (128, NCH, D), BF16) for nm in ["wq2", "wk2", "wv2"]}
        hbt = sb2("hbt", (128, 2, 4, D), BF16)
        hb_fm = sb2("hb_fm", (128, NCH, 512), BF16)
        k2t = sb2("k2t", (128, 4, D), BF16)
        vaug = sb2("vaug", (128, 4, H, 65), BF16)
        kfstage = sb2("kfstage", (64, H, 512), BF16)
        hbq = sb2("hbq", (128, 2, D), BF16)

        for nm in ["wq2", "wk2", "wv2"]:
            load_weight(w2[nm], nm)
        ldw_p2 = p.ctr["LDW"]
        p.wait("vector", "LDW", ldw_p2)
        for nm in ["wq2", "wk2", "wv2"]:
            for c in range(NCH):
                p.op("vector", lambda v, nm=nm, c=c: v.tensor_scalar_mul(
                    w2[nm][:, c, :], w2[nm][:, c, :], vecs["n1w"][:, c:c + 1]), "VE")
        p.drain("vector")
        foldw2 = p.inc("vector", "VE")
        p.wait("tensor", "VE", foldw2)
        for h in range(H):
            p.dma("sync", kfG[h, 64:65, :], di["onesrow"][:, :], "ST")
        gp_t = p.op("gpsimd", lambda g: g.memset(vaug[:, :, :, 64:65], 1.0), "GP")
        p.wait("vector", "GP", gp_t)
        p.wait("sync", "ST", p.ctr["ST"])

        hbsrc = hbG[:, :].rearrange("(s c p) d -> s p c d", p=128, c=4)
        hb_ld = {0: p.dma("sync", hbt[:, 0, :, :], hbsrc[0], lxs(0))}
        tr_mark = {}
        st_mark = {}
        sq_scr = hb_fm[:].rearrange("p a b -> p (a b)")
        k2t_flat = k2t[:].rearrange("p a b -> p (a b)")
        k2tv = k2t[:].rearrange("p a (h d) -> p d (a h)", d=HD)
        sqv = sq_scr.rearrange("p (x d) -> p x d", d=HD)
        s64b = stat64[:].unsqueeze(1).broadcast_to([128, HD, 4 * H])
        for G in range(NSUP):
            hbuf = G % 2
            if G + 1 < NSUP:
                if G - 1 in tr_mark:
                    p.wait("sync", "PE", tr_mark[G - 1])
                hb_ld[G + 1] = p.dma("sync", hbt[:, (G + 1) % 2, :, :], hbsrc[G + 1], lxs(G + 1))
            p.wait("tensor", lxs(G), hb_ld[G])
            for tc in range(4):
                transpose_tm_to_fm(lambda c, tc=tc, hbuf=hbuf:
                                   hbt[:, hbuf, tc, c * 128:(c + 1) * 128], hb_fm, tc)
            tr_mark[G] = p.ctr["PE"]
            trp.flush()
            if G - 1 in st_mark:
                p.wait("vector", "ST", st_mark[G - 1])
            for tc in range(4):
                for ot in range(2):
                    oi, bank = mmp.next()
                    def mmk(t, tc=tc, ot=ot, bank=bank):
                        for c in range(NCH):
                            r = t.matmul(bank[:], hb_fm[:, c, tc * 128:(tc + 1) * 128],
                                         w2["wk2"][:, c, ot * 512:(ot + 1) * 512],
                                         start=(c == 0), stop=(c == NCH - 1), skip_group_check=True)
                        return r
                    pe_t = p.op("tensor", lambda t, f=mmk: f(t), "PE")
                    p.wait("vector", "PE", pe_t)
                    ve_t = p.op("vector", lambda v, tc=tc, ot=ot, bank=bank:
                                v.tensor_copy(k2t[:, tc, ot * 512:(ot + 1) * 512], bank[:]), "VE")
                    mmp.done(oi, "VE", ve_t)
                for ot in range(2):
                    oi, bank = mmp.next()
                    def mmv(t, tc=tc, ot=ot, bank=bank):
                        for c in range(NCH):
                            r = t.matmul(bank[:], hb_fm[:, c, tc * 128:(tc + 1) * 128],
                                         w2["wv2"][:, c, ot * 512:(ot + 1) * 512],
                                         start=(c == 0), stop=(c == NCH - 1), skip_group_check=True)
                        return r
                    pe_t = p.op("tensor", lambda t, f=mmv: f(t), "PE")
                    p.wait("vector", "PE", pe_t)
                    ve_t = p.op("vector", lambda v, tc=tc, ot=ot, bank=bank:
                                v.tensor_copy(
                                    vaug[:, tc, ot * 8:(ot + 1) * 8, 0:64],
                                    bank[:, 0:512].rearrange("p (a b) -> p a b", a=8)), "VE")
                    mmp.done(oi, "VE", ve_t)
            p.wait("scalar", "VE", p.ctr["VE"])
            p.drain("vector")
            sc_t = p.op("scalar", lambda s: s.activation(
                sq_scr, k2t_flat, AF.Square), "SC")
            p.wait("vector", "SC", sc_t)
            def vrms(v):
                v.tensor_reduce(stat64[:], sqv, AX.X, ALU.add)
                v.drain()
                v.tensor_scalar_mul(stat64[:], stat64[:], 1.0 / HD)
                return v.drain()
            ve_t = p.op("vector", lambda v, f=vrms: f(v), "VE")
            p.wait("scalar", "VE", ve_t)
            sc_t = p.op("scalar", lambda s: s.activation(
                stat64[:], stat64[:], AF.Sqrt, bias=consts[:, 1:2]), "SC")
            p.wait("vector", "SC", sc_t)
            def vrms2(v):
                v.reciprocal(stat64[:], stat64[:])
                v.drain()
                v.tensor_tensor(k2tv, k2tv, s64b, ALU.mult)
                return v.drain()
            ve_t = p.op("vector", lambda v, f=vrms2: f(v), "VE")
            p.wait("tensor", "VE", ve_t)
            for tc in range(4):
                for hb_ in range(0, H, 8):
                    def evk(v, off, tc=tc, hb_=hb_):
                        return v.tensor_scalar_mul(
                            kfstage[0:64, hb_:hb_ + 8, tc * 128:(tc + 1) * 128],
                            pTR[0:64, off:off + 1024].rearrange("p (a b) -> p a b", a=8),
                            qnkn[0:64, 0:1])
                    tr_batch([k2t[:, tc, (hb_ + j) * 64:(hb_ + j + 1) * 64]
                              for j in range(8)], evk)
            p.wait("sync", "VE", p.ctr["VE"])
            p.dma("sync", kfG[:, :, :].rearrange("h p n -> p h n")[0:64, :, G * 512:(G + 1) * 512],
                  kfstage[:], "ST")
            for tc in range(4):
                st_mark[G] = p.dma(
                    "sync",
                    vG[:, G * 512 + tc * 128:G * 512 + (tc + 1) * 128, :].rearrange(
                        "h p a -> p h a"),
                    vaug[:, tc, :, :], "ST")

        p.wait("sync", "LDG", own_gathers)
        q_ld = {0: p.dma("sync", hbq[:, 0, :], hbOwnG[0:128, :], lxs(0))}
        q_tr = {}
        for i in range(S):
            qb = i % 2
            if i + 1 < S:
                if i - 1 in q_tr:
                    p.wait("sync", "PE", q_tr[i - 1])
                q_ld[i + 1] = p.dma("sync", hbq[:, (i + 1) % 2, :],
                                    hbOwnG[(i + 1) * 128:(i + 2) * 128, :], lxs(i + 1))
            p.wait("tensor", lxs(i), q_ld[i])
            def evq(v, off):
                return v.tensor_copy(
                    hb_fm[:, 0:NCH, 0:128],
                    pTR[:, 0:1024].rearrange("p (a b) -> p a b", a=NCH))
            tr_batch([hbq[:, qb, j * 128:(j + 1) * 128] for j in range(NCH)], evq)
            q_tr[i] = p.ctr["PE"]
            for ot in range(2):
                oi, bank = mmp.next()
                def mmq2(t, ot=ot, bank=bank):
                    for c in range(NCH):
                        r = t.matmul(bank[:], hb_fm[:, c, 0:128],
                                     w2["wq2"][:, c, ot * 512:(ot + 1) * 512],
                                     start=(c == 0), stop=(c == NCH - 1), skip_group_check=True)
                    return r
                pe_t = p.op("tensor", lambda t, f=mmq2: f(t), "PE")
                p.wait("vector", "PE", pe_t)
                ve_t = p.op("vector", lambda v, i=i, ot=ot, bank=bank: v.tensor_copy(
                    q2n[:, i, ot * 512:(ot + 1) * 512], bank[:]), "VE")
                mmp.done(oi, "VE", ve_t)
        for g in range(0, S, 4):
            n = min(4, S - g)
            qflat = unionA[:, g * D:(g + n) * D]
            qv = qflat.rearrange("p (x d) -> p d x", d=HD)
            sqq = sq_scr[:, 0:n * D].rearrange("p (x d) -> p x d", d=HD)
            snb = stat64[:, 0:n * H].unsqueeze(1).broadcast_to([128, HD, n * H])
            p.wait("scalar", "VE", p.ctr["VE"])
            p.drain("vector")
            sc_t = p.op("scalar", lambda s, qflat=qflat, n=n: s.activation(
                sq_scr[:, 0:n * D], qflat, AF.Square), "SC")
            p.wait("vector", "SC", sc_t)
            def vrq(v, sqq=sqq, n=n):
                v.tensor_reduce(stat64[:, 0:n * H], sqq, AX.X, ALU.add)
                v.drain()
                v.tensor_scalar_mul(stat64[:, 0:n * H], stat64[:, 0:n * H], 1.0 / HD)
                return v.drain()
            ve_t = p.op("vector", lambda v, f=vrq: f(v), "VE")
            p.wait("scalar", "VE", ve_t)
            sc_t = p.op("scalar", lambda s, n=n: s.activation(
                stat64[:, 0:n * H], stat64[:, 0:n * H], AF.Sqrt, bias=consts[:, 1:2]), "SC")
            p.wait("vector", "SC", sc_t)
            def vrq2(v, qv=qv, snb=snb, n=n):
                v.reciprocal(stat64[:, 0:n * H], stat64[:, 0:n * H])
                v.drain()
                v.tensor_tensor(qv, qv, snb, ALU.mult)
                return v.drain()
            p.op("vector", lambda v, f=vrq2: f(v), "VE")

        if upto <= 2:
            p.barrier()
            p.wait("sync", "ST", p.ctr["ST"])
            p.op("gpsimd", lambda g: g.memset(hbq[:, 0, :], 0.0), "GP")
            p.wait("sync", "GP", p.ctr["GP"])
            for _i in range(S):
                p.dma("sync", out_hbm[_i * 128:(_i + 1) * 128, 0:512],
                      hbq[:, 0, :].bitcast(F32), "ST")
            p.wait("sync", "ST", p.ctr["ST"])
            with nc.Block() as block:
                p.finalize(block)
            p2ctx.close()
            return nc
        p.barrier()
        p.wait("sync", "ST", p.ctr["ST"])
        p2ctx.close()

        p3ctx = ExitStack()
        sb3 = lambda name, shape, dt: mksb(p3ctx, name, shape, dt)
        KF = [sb3(f"KF{j}", (65, N), BF16) for j in range(2)]
        Vh = [sb3(f"Vh{j}", (128, n_chunks, 65), BF16) for j in range(2)]
        Qa = sb3("Qa", (65, T_OWN), BF16)
        Pbuf = sb3("Pbuf", (128, T_OWN), BF16)
        avsb = sb3("avsb", (65, 512), BF16)
        mxs = sb3("mxs", (128, 8), F32)
        negm = sb3("negm", (128, 1), F32)
        rsp = sb3("rsp", (128, 4), F32)

        def i0_of(j):
            return j // 2

        p3p = BankPool(p, [pA, pB, pC])
        pe_consumed = {}
        lds_mark = {}

        def issue_head_loads(h):
            sem = "LK0" if h % 2 == 0 else "LK1"
            if h >= 2:
                p.wait("sync", "PE", pe_consumed[h - 2])
            p.dma("sync", KF[h % 2][:], kfG[h, :, :], sem)
            lds_mark[h] = (sem, p.dma("sync", Vh[h % 2][:],
                                vG[h, :, :].rearrange("(c p) a -> p c a", p=128), sem))

        p.op("vector", lambda v: v.tensor_copy(
            Qa[64:65, 0:T_OWN].rearrange("p (t o) -> p t o", o=1),
            negmh[0:1, 0:1].unsqueeze(1).broadcast_to([1, T_OWN, 1])), "VE")
        p.drain("vector")

        issue_head_loads(0)
        for h in range(H):
            buf = h % 2
            if h + 1 < H:
                issue_head_loads(h + 1)
            p.wait("tensor", lds_mark[h][0], lds_mark[h][1])
            p.wait("scalar", lds_mark[h][0], lds_mark[h][1])
            p.wait("vector", lds_mark[h][0], lds_mark[h][1])
            for i in range(0, S, 4):
                nsl = min(4, S - i)
                def evqa(v, off, i=i, nsl=nsl):
                    return v.tensor_scalar_mul(
                        Qa[0:64, i * 128:(i + nsl) * 128],
                        pTR[0:64, off:off + nsl * 128], 0.125)
                tr_batch([q2n[:, i + j, h * 64:(h + 1) * 64] for j in range(nsl)], evqa)
            trp.flush()
            p.wait("tensor", "VE", p.ctr["VE"])
            av_started = [False] * NT
            prev_av = None

            def evac_bank(t_i, h=h):
                cols = min(512, T_OWN - 512 * t_i)
                p.wait("vector", "PE", p.ctr["PE"])
                ve_t = p.op("vector", lambda v, t_i=t_i, cols=cols: v.tensor_copy(
                    avsb[:, 0:cols], avP[t_i][0:65, 0:cols]), "VE")
                p.drain("vector")
                p.wait("tensor", "VE", ve_t)
                nblk = -(-cols // 128)
                oi, bank = p3p.next()
                def trdiv_pe(t, cols=cols, nblk=nblk, bank=bank):
                    for blk in range(nblk):
                        nq = min(128, cols - blk * 128)
                        r = t.transpose(bank[:].bitcast(BF16)[0:nq, blk * 256:blk * 256 + 65],
                                        avsb[:, blk * 128:blk * 128 + nq],
                                        ident[0:65, 0:65])
                    return r
                pe_t = p.op("tensor", lambda t, f=trdiv_pe: f(t), "PE")
                p.wait("vector", "PE", pe_t)
                def vdiv(v, t_i=t_i, h=h, nblk=nblk, bank=bank):
                    bv = bank[:].bitcast(BF16)
                    den = bv.rearrange("p (s f) -> p s f", f=256)[:, 0:nblk, 64:65]
                    v.reciprocal(rsp[:, 0:nblk], den.rearrange("p s f -> p (s f)"))
                    v.drain()
                    pvT = bv.rearrange("p (s f) -> p f s", f=256)[:, 0:64, 0:nblk]
                    aoT = attnout[:, 4 * t_i:4 * t_i + nblk, h * 64:(h + 1) * 64].rearrange(
                        "p s f -> p f s")
                    rb = rsp[:, 0:nblk].unsqueeze(1).broadcast_to([128, 64, nblk])
                    v.tensor_tensor(aoT, pvT, rb, ALU.mult)
                    return v.drain()
                ve_t = p.op("vector", lambda v, f=vdiv: f(v), "VE")
                p3p.done(oi, "VE", ve_t)

            def emit_prev_av(prev):
                (jp, ap_, bp_, sem_p, val_p) = prev
                t_i = ap_ // 512
                st_flag = not av_started[t_i]
                av_started[t_i] = True
                last_j = min(8 * t_i + 7, n_chunks - 1)
                p.wait("tensor", sem_p, val_p)
                def mmav2(t, jp=jp, ap_=ap_, bp_=bp_, t_i=t_i, st_flag=st_flag,
                          last_j=last_j, buf=buf):
                    return t.matmul(avP[t_i][0:65, (ap_ - 512 * t_i):(bp_ - 512 * t_i)],
                                    Vh[buf][:, jp, :], Pbuf[:, ap_:bp_],
                                    start=st_flag, stop=(jp == last_j),
                                    skip_group_check=True)
                p.op("tensor", lambda t, f=mmav2: f(t), "PE")
                if jp == last_j:
                    evac_bank(t_i)

            for j in range(n_chunks):
                q0 = 128 * i0_of(j)
                a = q0
                while a < T_OWN:
                    b_ = min((a // 512 + 1) * 512, T_OWN)
                    oi, bank = p3p.next()
                    def mmkm(t, j=j, a=a, b_=b_, bank=bank, buf=buf):
                        return t.matmul(bank[:, 0:b_ - a], KF[buf][:, j * 128:(j + 1) * 128],
                                        Qa[:, a:b_], start=True, stop=True, skip_group_check=True)
                    p.op("tensor", lambda t, f=mmkm: f(t), "PE")
                    if prev_av is not None:
                        emit_prev_av(prev_av)
                        prev_av = None
                    km_mark = p.inc("tensor", "PE")
                    p.wait("scalar", "PE", km_mark)
                    sc_t = p.op("scalar", lambda s, a=a, b_=b_, bank=bank: s.activation(
                        Pbuf[:, a:b_], bank[:, 0:b_ - a], AF.Exp), "SC")
                    p3p.done(oi, "SC", sc_t)
                    if a == q0:
                        p.wait("vector", "SC", sc_t)
                        ve_t = p.op("vector", lambda v, j=j, a=a: v.tensor_tensor(
                            Pbuf[:, a:a + 128], Pbuf[:, a:a + 128],
                            dmask[:, j % 2, :], ALU.mult), "VE")
                        p.drain("vector")
                        prev_av = (j, a, b_, "VE", ve_t)
                    else:
                        prev_av = (j, a, b_, "SC", sc_t)
                    a = b_
            if prev_av is not None:
                emit_prev_av(prev_av)
                prev_av = None
            pe_consumed[h] = p.ctr["PE"]

        if upto <= 3:
            p.barrier()
            p.wait("sync", "ST", p.ctr["ST"])
            p.op("gpsimd", lambda g: g.memset(Pbuf[:], 0.0), "GP")
            p.wait("sync", "GP", p.ctr["GP"])
            for _i in range(S):
                p.dma("sync", out_hbm[_i * 128:(_i + 1) * 128, 0:T_OWN // 2],
                      Pbuf[:].bitcast(F32), "ST")
            p.wait("sync", "ST", p.ctr["ST"])
            with nc.Block() as block:
                p.finalize(block)
            p3ctx.close()
            return nc
        p.barrier()
        p3ctx.close()

        p4actx = ExitStack()
        sb4a = lambda name, shape, dt: mksb(p4actx, name, shape, dt)
        wo2s = sb4a("wo2s", (128, NCH, D), BF16)
        gws = sb4a("gws", (128, NCH, 4), BF16)
        ao_fm = sb4a("ao_fm", (128, NCH, 128), BF16)
        y1q = sb4a("y1q", (128, 2, D), F32)
        x2b = sb4a("x2b", (128, 2, D), F32)
        xfb = sb4a("xfb", (128, D), BF16)
        lgall = sb4a("lgall", (128, 4 * S), F32)
        tmp4a = sb4a("tmp4a", (128, 4 * S), F32)
        m1g = sb4a("m1g", (128, S), F32)
        m2g = sb4a("m2g", (128, S), F32)
        ssum = sb4a("ssum", (128, S), F32)

        xf_fm = unionA[:].rearrange("p (c t) -> p c t", c=NCH)

        n2wtok_sb = sb4a("n2wtok_sb", (128, D), F32)
        load_weight(wo2s, "wo2")
        load_weight(gws, "gw")
        p.dma("gpsimd", n2wtok_sb[:], di["n2wtok"][:, :], "LDW")
        ldw4 = p.ctr["LDW"]
        p.wait("tensor", "LDW", ldw4)
        p.wait("vector", "LDW", ldw4)

        p.wait("sync", "LDG", own_gathers)
        y1_ld = {0: p.dma("sync", y1q[:, 0, :], y1OwnG[0:128, :], lxs(0))}
        x2_st = {}
        x2_sq = {}
        y1_done = {}
        for i in range(S):
            xb2 = i % 2
            if i + 1 < S:
                if i - 1 in y1_done:
                    p.wait("sync", "VE", y1_done[i - 1])
                y1_ld[i + 1] = p.dma("sync", y1q[:, (i + 1) % 2, :],
                                     y1OwnG[(i + 1) * 128:(i + 2) * 128, :], lxs(i + 1))
            if i - 2 in x2_st:
                p.wait("vector", "SX0" if i % 2 == 0 else "SX1", x2_st[i - 2])
                p.wait("vector", "SC", x2_sq[i - 2])
            def eva(v, off, i=i):
                return v.tensor_copy(
                    ao_fm[:, 0:NCH, :],
                    pTR[:, 0:1024].rearrange("p (a b) -> p a b", a=NCH))
            tr_batch([attnout[:, i, jj * 128:(jj + 1) * 128] for jj in range(NCH)], eva)
            p.wait("vector", lxs(i), y1_ld[i])
            for ot, (o0, o1) in enumerate([(256, 768), (768, 1024)]):
                oi, bank = mmp.next()
                def mmo(t, i=i, o0=o0, o1=o1, bank=bank):
                    for c in range(NCH):
                        r = t.matmul(bank[:, 0:o1 - o0], ao_fm[:, c, :],
                                     wo2s[:, c, o0:o1],
                                     start=(c == 0), stop=(c == NCH - 1), skip_group_check=True)
                    return r
                pe_t = p.op("tensor", lambda t, f=mmo: f(t), "PE")
                p.wait("vector", "PE", pe_t)
                ve_t = p.op("vector", lambda v, o0=o0, o1=o1, bank=bank, xb2=xb2, i=i:
                            v.scalar_tensor_tensor(x2b[:, xb2, o0:o1], bank[:, 0:o1 - o0], 1.0,
                                                   y1q[:, i % 2, o0:o1], ALU.mult, ALU.add), "VE")
                mmp.done(oi, "VE", ve_t)
            ve_t = p.op("vector", lambda v, xb2=xb2, i=i: v.tensor_copy(
                x2b[:, xb2, 0:256], y1q[:, i % 2, 0:256]), "VE")
            y1_done[i] = ve_t
            p.wait("sync", "VE", ve_t)
            x2_st[i] = p.dma("sync", x2G[i * 128:(i + 1) * 128, :], x2b[:, xb2, :],
                             "SX0" if i % 2 == 0 else "SX1")
            p.wait("scalar", "VE", ve_t)
            sc_t = p.op("scalar", lambda s, xb2=xb2: s.activation(
                bf_scr[:, 0:D], x2b[:, xb2, :], AF.Square, accum_out=stat["sumsq"][:, 2:3]), "SC")
            x2_sq[i] = sc_t
            p.wait("vector", "SC", sc_t)
            def vxf(v):
                v.tensor_scalar_mul(stat["tmp"][:, 2:3], stat["sumsq"][:, 2:3], 1.0 / D)
                return v.drain()
            ve_t = p.op("vector", lambda v, f=vxf: f(v), "VE")
            p.wait("scalar", "VE", ve_t)
            sc_t = p.op("scalar", lambda s: s.activation(
                stat["tmp2"][:, 2:3], stat["tmp"][:, 2:3], AF.Sqrt, bias=consts[:, 1:2]), "SC")
            p.wait("vector", "SC", sc_t)
            def vxf2(v, xb2=xb2):
                v.reciprocal(stat["r"][:, 2:3], stat["tmp2"][:, 2:3])
                v.drain()
                v.scalar_tensor_tensor(xfb[:], x2b[:, xb2, :], stat["r"][:, 2:3],
                                       n2wtok_sb[:], ALU.mult, ALU.mult)
                return v.drain()
            ve_t = p.op("vector", lambda v, f=vxf2: f(v), "VE")
            p.wait("tensor", "VE", ve_t)
            def evx(v, off, i=i):
                return v.tensor_copy(
                    xf_fm[:, 0:NCH, i * 128:(i + 1) * 128],
                    pTR[:, 0:1024].rearrange("p (a b) -> p a b", a=NCH))
            tr_batch([xfb[:, jj * 128:(jj + 1) * 128] for jj in range(NCH)], evx)
            def mmgate(t, i=i):
                for c in range(NCH):
                    r = t.matmul(pC[:, i * 4:(i + 1) * 4], xf_fm[:, c, i * 128:(i + 1) * 128],
                                 gws[:, c, :], start=(c == 0), stop=(c == NCH - 1),
                                 skip_group_check=True)
                return r
            pe_t = p.op("tensor", lambda t, f=mmgate: f(t), "PE")
            gate_pe_last = pe_t

        lg3 = lgall[:].rearrange("p (s e) -> p s e", s=S)
        lgT = lgall[:].rearrange("p (s e) -> p e s", s=S)
        t4T = tmp4a[:].rearrange("p (s e) -> p e s", s=S)
        t43 = tmp4a[:].rearrange("p (s e) -> p s e", s=S)
        wts3 = wts[:, :, :]
        wtsT = wts_t[:].rearrange("p s e -> p e s")
        m1m = m1g[:].unsqueeze(1).broadcast_to([128, 4, S])
        m2m = m2g[:].unsqueeze(1).broadcast_to([128, 4, S])
        ssm = ssum[:].unsqueeze(1).broadcast_to([128, 4, S])
        p.wait("vector", "PE", gate_pe_last)
        def vtop_a(v):
            v.tensor_copy(lgall[:], pC[:, 0:4 * S])
            v.drain()
            v.tensor_reduce(m1g[:], lg3, AX.X, ALU.max)
            v.drain()
            v.tensor_tensor(lgT, lgT, m1m, ALU.subtract)
            return v.drain()
        ve_t = p.op("vector", lambda v, f=vtop_a: f(v), "VE")
        p.wait("scalar", "VE", ve_t)
        p.wait("tensor", "VE", ve_t)
        sc_t = p.op("scalar", lambda s: s.activation(lgall[:], lgall[:], AF.Exp), "SC")
        p.wait("vector", "SC", sc_t)
        def vtop_b(v):
            v.tensor_reduce(ssum[:], lg3, AX.X, ALU.add)
            v.drain()
            v.reciprocal(ssum[:], ssum[:])
            v.drain()
            v.tensor_tensor(lgT, lgT, ssm, ALU.mult)
            v.drain()
            v.tensor_reduce(m1g[:], lg3, AX.X, ALU.max)
            v.drain()
            v.tensor_tensor(t4T, lgT, m1m, ALU.is_equal)
            v.drain()
            v.scalar_tensor_tensor(tmp4a[:], tmp4a[:], -1e30, lgall[:], ALU.mult, ALU.add)
            v.drain()
            v.tensor_reduce(m2g[:], t43, AX.X, ALU.max)
            v.drain()
            v.tensor_tensor(wtsT, lgT, m2m, ALU.is_ge)
            v.drain()
            v.tensor_tensor(wts3, wts3, lg3, ALU.mult)
            v.drain()
            v.tensor_reduce(ssum[:], wts3, AX.X, ALU.add)
            v.drain()
            v.tensor_scalar(ssum[:], ssum[:], 1e-8, None, ALU.add)
            v.drain()
            v.reciprocal(ssum[:], ssum[:])
            v.drain()
            v.tensor_tensor(wtsT, wtsT, ssm, ALU.mult)
            return v.drain()
        ve_t = p.op("vector", lambda v, f=vtop_b: f(v), "VE")

        if "x2" in dbg:
            p.wait("sync", "ST", p.ctr["ST"])
            p.wait("sync", "SX0", p.ctr["SX0"])
            p.wait("sync", "SX1", p.ctr["SX1"])
            p.wait("sync", "SC", p.ctr["SC"])
            p.wait("sync", "VE", p.ctr["VE"])
            for i in range(S):
                ld = p.dma("sync", x2b[:, 0, :], x2G[i * 128:(i + 1) * 128, :], "LDS")
                p.wait("sync", "LDS", ld)
                p.dma("sync", dbg_t["x2"][i * 128:(i + 1) * 128, :], x2b[:, 0, :], "ST")
                p.wait("sync", "ST", p.ctr["ST"])
        if "wts" in dbg:
            p.wait("sync", "VE", p.ctr["VE"])
            for i in range(S):
                p.dma("sync", dbg_t["wts"][i * 128:(i + 1) * 128, :], wts[:, i, :], "ST")
            p.wait("sync", "ST", p.ctr["ST"])

        p.barrier()
        p4actx.close()
        attnctx.close()

        p4bctx = ExitStack()
        sb4b = lambda name, shape, dt: mksb(p4bctx, name, shape, dt)
        TH = T_OWN // 2
        NTT = -(-TH // 512)
        h1 = sb4b("h1", (128, HID // 128, TH), BF16)
        h2 = sb4b("h2", (128, HID // 128, TH), BF16)
        moe = sb4b("moe", (128, 6, TH), F32)
        wstr = [sb4b(f"wstr{j}", (128, HID // 128, 128), BF16) for j in range(4)]
        wdn = sb4b("wdn", (128, HID // 128, 768), BF16)
        y1q2 = sb4b("y1q2", (128, D), F32)
        x2rb = sb4b("x2rb", (128, D), F32)
        rbuf = x2rb
        obuf = y1q2

        last_wbuf_pe = {0: None, 1: None, 2: None, 3: None}
        last_wdn_pe = {0: None, 1: None}
        WSEM = ["LW0", "LW1", "LW4", "LW5"]

        def stream_w(name, oc, buf, in_chunks, gate_pe=None):
            sem = WSEM[buf]
            if gate_pe is not None:
                p.wait("gpsimd", "PE", gate_pe)
            src = di[name][:, oc * 128:(oc + 1) * 128].rearrange("(c p) o -> p c o", p=128)
            tgt = p.dma("gpsimd", wstr[buf][:, 0:in_chunks, :], src, sem)
            return (sem, tgt)

        def proj_A(name, in_fm, in_chunks, out_chunks, dst_fm, act, half, dep=None):
            last_ev = None
            for oc in range(out_chunks):
                buf = oc % 4
                sem, tgt = stream_w(name, oc, buf, in_chunks,
                                    gate_pe=last_wbuf_pe[buf])
                p.wait("tensor", sem, tgt)
                if dep is not None:
                    p.wait("tensor", dep[0], dep[1])
                    dep = None
                for tt in range(NTT):
                    ncols = min(512, TH - tt * 512)
                    oi, bank = mmp.next()
                    def mmA(t, oc=oc, buf=buf, tt=tt, ncols=ncols, bank=bank,
                            in_fm=in_fm, in_chunks=in_chunks):
                        for c in range(in_chunks):
                            r = t.matmul(bank[:, 0:ncols], wstr[buf][:, c, :],
                                         in_fm[:, c, tt * 512:tt * 512 + ncols],
                                         start=(c == 0), stop=(c == in_chunks - 1),
                                         skip_group_check=True)
                        return r
                    pe_t = p.op("tensor", lambda t, f=mmA: f(t), "PE")
                    if act is not None:
                        p.wait("scalar", "PE", pe_t)
                        sc_t = p.op("scalar", lambda s, oc=oc, tt=tt, ncols=ncols, bank=bank,
                                    dst_fm=dst_fm, act=act: s.activation(
                                        dst_fm[:, oc, tt * 512:tt * 512 + ncols],
                                        bank[:, 0:ncols], act), "SC")
                        mmp.done(oi, "SC", sc_t)
                        last_ev = ("SC", sc_t)
                    else:
                        p.wait("vector", "PE", pe_t)
                        ve_t = p.op("vector", lambda v, oc=oc, tt=tt, ncols=ncols, bank=bank,
                                    dst_fm=dst_fm: v.tensor_copy(
                                        dst_fm[:, oc, tt * 512:tt * 512 + ncols],
                                        bank[:, 0:ncols]), "VE")
                        mmp.done(oi, "VE", ve_t)
                        last_ev = ("VE", ve_t)
                last_wbuf_pe[buf] = p.ctr["PE"]
            return last_ev

        def proj_B_final(name, in_fm, in_chunks, e_idx, half, dep=None):
            half0 = half * (S // 2)
            tgts = {}
            for ot in range(2):
                o0, o1 = (0, 512) if ot == 0 else (512, 768)
                wsem = "LW2" if ot == 0 else "LW3"
                if last_wdn_pe[ot] is not None:
                    p.wait("gpsimd", "PE", last_wdn_pe[ot])
                tgts[ot] = p.dma("gpsimd", wdn[:, 0:in_chunks, ot * 512:ot * 512 + (o1 - o0)],
                                 di[name][:, 256 + o0:256 + o1].rearrange("(c p) o -> p c o", p=128),
                                 wsem)
            if dep is not None:
                p.wait("tensor", dep[0], dep[1])
            for ot in range(2):
                o0, o1 = (0, 512) if ot == 0 else (512, 768)
                wsem = "LW2" if ot == 0 else "LW3"
                p.wait("tensor", wsem, tgts[ot])
                for sl in range(S // 2):
                    i_glob = half0 + sl
                    oi, bank = mmp.next()
                    def mmB(t, sl=sl, ot=ot, o0=o0, o1=o1, bank=bank, in_fm=in_fm, in_chunks=in_chunks):
                        for c in range(in_chunks):
                            r = t.matmul(bank[:, 0:o1 - o0],
                                         in_fm[:, c, sl * 128:(sl + 1) * 128],
                                         wdn[:, c, ot * 512:ot * 512 + (o1 - o0)],
                                         start=(c == 0), stop=(c == in_chunks - 1),
                                         skip_group_check=True)
                        return r
                    pe_t = p.op("tensor", lambda t, f=mmB: f(t), "PE")
                    p.wait("vector", "PE", pe_t)
                    oc0 = o0 // 128
                    noc = (o1 - o0) // 128
                    if e_idx == 0:
                        def evB(v, sl=sl, oc0=oc0, noc=noc, bank=bank, i_glob=i_glob, e_idx=e_idx):
                            return v.tensor_scalar_mul(
                                moe[:, oc0:oc0 + noc, sl * 128:(sl + 1) * 128],
                                bank[:, 0:noc * 128].rearrange("p (a b) -> p a b", a=noc),
                                wts[:, i_glob, e_idx:e_idx + 1])
                    else:
                        def evB(v, sl=sl, oc0=oc0, noc=noc, bank=bank, i_glob=i_glob, e_idx=e_idx):
                            v.scalar_tensor_tensor(
                                moe[:, oc0:oc0 + noc, sl * 128:(sl + 1) * 128],
                                bank[:, 0:noc * 128].rearrange("p (a b) -> p a b", a=noc),
                                wts[:, i_glob, e_idx:e_idx + 1],
                                moe[:, oc0:oc0 + noc, sl * 128:(sl + 1) * 128],
                                ALU.mult, ALU.add)
                            return v.drain()
                    ve_t = p.op("vector", lambda v, f=evB: f(v), "VE")
                    mmp.done(oi, "VE", ve_t)
                last_wdn_pe[ot] = p.ctr["PE"]

        for half in range(2):
            base = half * TH
            xf_half = xf_fm[:, :, base:base + TH]
            r = proj_A("e0u", xf_half, NCH, HID // 128, h1, AF.Gelu, half)
            proj_B_final("e0d", h1, HID // 128, 0, half, dep=r)
            r1 = proj_A("e1a", xf_half, NCH, HID // 128, h1, AF.Silu, half)
            proj_A("e1b", xf_half, NCH, HID // 128, h2, None, half)
            p.wait("vector", r1[0], r1[1])
            def vmul(v):
                v.drain()
                v.tensor_tensor(h1[:], h1[:], h2[:], ALU.mult)
                return v.drain()
            ve_t = p.op("vector", lambda v, f=vmul: f(v), "VE")
            p.wait("tensor", "VE", ve_t)
            proj_B_final("e1d", h1, HID // 128, 1, half)
            r = proj_A("e2a", xf_half, NCH, HID // 128, h1, AF.Gelu, half)
            r = proj_A("e2b", h1, HID // 128, HID // 128, h2, AF.Gelu, half, dep=r)
            r = proj_A("e2c", h2, HID // 128, HID // 128, h1, AF.Gelu, half, dep=r)
            proj_B_final("e2d", h1, HID // 128, 2, half, dep=r)
            r = proj_A("e3a", xf_half, NCH, NCH, h1, AF.Gelu, half)
            r = proj_A("e3b", h1, NCH, HID // 128, h2, AF.Gelu, half, dep=r)
            proj_B_final("e3c", h2, HID // 128, 3, half, dep=r)

            p.wait("sync", "VE", p.ctr["VE"])
            p.wait("sync", "ST", p.ctr["ST"])
            p.wait("sync", "SX0", p.ctr["SX0"])
            p.wait("sync", "SX1", p.ctr["SX1"])
            for sl in range(S // 2):
                i_glob = half * (S // 2) + sl
                ldx = p.dma("sync", x2rb[:], x2G[i_glob * 128:(i_glob + 1) * 128, :], "LDS")
                p.wait("sync", "LDS", ldx)
                ldy = p.dma("sync", y1q2[:], y1OwnG[i_glob * 128:(i_glob + 1) * 128, :], "LDS")
                p.wait("sync", "LDS", ldy)
                p.wait("vector", "LDS", ldy)
                def vr2(v, sl=sl):
                    v.tensor_tensor(rbuf[:], x2rb[:], y1q2[:], ALU.add)
                    v.drain()
                    v.tensor_tensor(
                        rbuf[:, 256:1024].rearrange("p (a b) -> p a b", a=6),
                        rbuf[:, 256:1024].rearrange("p (a b) -> p a b", a=6),
                        moe[:, 0:6, sl * 128:(sl + 1) * 128], ALU.add)
                    return v.drain()
                ve_t = p.op("vector", lambda v, f=vr2: f(v), "VE")
                stats_ln(rbuf[:], 3, eps_ln_col)
                def vout(v):
                    v.tensor_scalar(obuf[:], rbuf[:], stat["mun"][:, 3:4],
                                    stat["r"][:, 3:4], ALU.add, ALU.mult)
                    return v.drain()
                ve_t = p.op("vector", lambda v, f=vout: f(v), "VE")
                p.wait("sync", "VE", ve_t)
                p.dma("sync", out_hbm[i_glob * 128:(i_glob + 1) * 128, :], obuf[:], "ST")
                p.wait("vector", "ST", p.ctr["ST"])
                p.wait("gpsimd", "ST", p.ctr["ST"])

        p.wait("sync", "ST", p.ctr["ST"])
        p4bctx.close()

        with nc.Block() as block:
            p.finalize(block)

    return nc


N_CHUNKS = 32


def kernel(**inputs):
    nc = build_program(N_CHUNKS)
    in_maps = []
    for b in range(4):
        for parity in range(2):
            in_maps.append(prep_core_inputs(inputs, b, parity, N_CHUNKS))
    res = run_bass_kernel_spmd(nc, in_maps, core_ids=list(range(8)))
    outs = [np.asarray(r["out"], dtype=np.float32) for r in res.results]
    return assemble_output(outs, N_CHUNKS)


# version 15
# speedup vs baseline: 1.0075x; 1.0075x over previous
import numpy as np
import ml_dtypes
from contextlib import ExitStack

import concourse.bass as bass
import concourse.mybir as mybir
from concourse.bass_utils import run_bass_kernel_spmd

F32 = mybir.dt.float32
BF16 = mybir.dt.bfloat16
AF = mybir.ActivationFunctionType
ALU = mybir.AluOpType
AX = mybir.AxisListType

D = 1024
H = 16
HD = 64
HID = 2048
NCH = 8
EPS_LN = 1e-5
EPS_RMS = 1.1920929e-07
BASE_ROPE = 10000.0
M_ZERO = 256


class Prog:

    ENGINES = ("sync", "gpsimd", "tensor", "vector", "scalar")

    def __init__(self, nc):
        self.nc = nc
        self.q = {e: [] for e in self.ENGINES}
        self.sems = {}
        self.ctr = {}

    def sem(self, name, ctx):
        h = ctx.enter_context(self.nc.semaphore(name))
        self.sems[name] = h
        self.ctr[name] = 0
        return h

    def emit(self, eng, fn):
        self.q[eng].append(fn)

    def dma(self, eng, out, in_, sem, **kw):
        h = self.sems[sem]
        self.ctr[sem] += 16
        tgt = self.ctr[sem]

        def f(e, out=out, in_=in_, h=h, kw=kw):
            e.dma_start(out, in_, **kw).then_inc(h, 16)
        self.emit(eng, f)
        return tgt

    def wait(self, eng, sem, val):
        h = self.sems[sem]

        def f(e, h=h, val=val):
            e.wait_ge(h, val)
        self.emit(eng, f)

    def inc(self, eng, sem, k=1):
        h = self.sems[sem]
        self.ctr[sem] += k
        tgt = self.ctr[sem]

        def f(e, h=h, k=k):
            e.drain().then_inc(h, k)
        self.emit(eng, f)
        return tgt

    def op(self, eng, fn, sem=None, k=1):
        if sem is None:
            self.emit(eng, fn)
            return None
        h = self.sems[sem]
        self.ctr[sem] += k
        tgt = self.ctr[sem]

        def f(e, fn=fn, h=h, k=k):
            r = fn(e)
            if r is None:
                r = e.drain()
            r.then_inc(h, k)
        self.emit(eng, f)
        return tgt

    def drain(self, eng):
        self.emit(eng, lambda e: e.drain())

    def barrier(self):
        self.ctr["BAR"] += len(self.ENGINES)
        tgt = self.ctr["BAR"]
        h = self.sems["BAR"]
        for eng in self.ENGINES:
            def f(e, h=h, tgt=tgt):
                e.drain().then_inc(h, 1)
                e.wait_ge(h, tgt)
            self.emit(eng, f)

    def finalize(self, block):
        for eng in self.ENGINES:
            fns = self.q[eng]

            def run(e, fns=fns):
                for f in fns:
                    f(e)
            getattr(block, eng)(run)


class BankPool:

    def __init__(self, p, banks, eager=False):
        self.p = p
        self.banks = banks
        self.pending = [None] * len(banks)
        self.i = 0
        self.eager = eager

    def next(self):
        i = self.i % len(self.banks)
        self.i += 1
        w = self.pending[i]
        if w is not None:
            self.p.wait("tensor", w[0], w[1])
            self.pending[i] = None
        return i, self.banks[i]

    def done(self, i, sem, val):
        if self.eager:
            self.p.wait("tensor", sem, val)
        else:
            self.pending[i] = (sem, val)

    def flush(self, eng="tensor"):
        for i, w in enumerate(self.pending):
            if w is not None:
                self.p.wait(eng, w[0], w[1])
                self.pending[i] = None


def _ceil_div(a, b):
    return -(-a // b)



def rope_tables_tm(n_chunks):
    inv = 1.0 / BASE_ROPE ** (np.arange(0, HD, 2, dtype=np.float64) / HD)
    cos = np.zeros((128, n_chunks, HD), np.float32)
    sinS = np.zeros((128, n_chunks, HD), np.float32)
    for g in range(n_chunks):
        pos = (g * 128 + np.arange(128, dtype=np.float64))[:, None]
        ang = pos * inv[None, :]
        c = np.cos(ang).astype(np.float32)
        s = np.sin(ang).astype(np.float32)
        cos[:, g, 0:32] = c
        cos[:, g, 32:64] = c
        sinS[:, g, 0:32] = -s
        sinS[:, g, 32:64] = s
    return (cos.reshape(128, -1).astype(ml_dtypes.bfloat16),
            sinS.reshape(128, -1).astype(ml_dtypes.bfloat16))


def parity_masks(parity):
    k = np.arange(128)[:, None]
    qq = np.arange(128)[None, :]
    tri = (k <= qq).astype(np.float32)
    m = np.zeros((128, 2, 128), np.float32)
    if parity == 0:
        m[:, 0, :] = tri
        m[:, 1, :] = 0.0
    else:
        m[:, 0, :] = 1.0
        m[:, 1, :] = tri
    return m.astype(ml_dtypes.bfloat16)


def triu_mask():
    k = np.arange(128)[:, None]
    qq = np.arange(128)[None, :]
    return (k <= qq).astype(ml_dtypes.bfloat16)


def chunk_major(v):
    return np.ascontiguousarray(v.reshape(NCH, 128).T)



def prep_core_inputs(inp, b, parity, n_chunks):
    N = n_chunks * 128
    f32 = np.float32
    bf = ml_dtypes.bfloat16
    m = {}
    m["x"] = np.ascontiguousarray(np.asarray(inp["L0_attn_x"])[b, :N, :], dtype=f32)
    m["wq1"] = np.asarray(inp["L0_attn_Wq"], f32)
    m["wk1"] = np.asarray(inp["L0_attn_Wk"], f32)
    m["wv1"] = np.asarray(inp["L0_attn_Wv"], f32)
    m["wo1"] = np.asarray(inp["L0_attn_Wo"], f32)
    m["wg1"] = np.asarray(inp["L0_attn_Wg1"], f32)
    m["wg2"] = np.asarray(inp["L0_attn_Wg2"], f32)
    for nm, key in [("wq2", "L0_ffn_wq"), ("wk2", "L0_ffn_wk"), ("wv2", "L0_ffn_wv"),
                    ("wo2", "L0_ffn_wo"), ("gw", "L0_ffn_gate_w"),
                    ("e0u", "L0_ffn_e0_up"), ("e0d", "L0_ffn_e0_down"),
                    ("e1a", "L0_ffn_e1_w1"), ("e1b", "L0_ffn_e1_w2"),
                    ("e1d", "L0_ffn_e1_down"),
                    ("e2a", "L0_ffn_e2_l1"), ("e2b", "L0_ffn_e2_l2"),
                    ("e2c", "L0_ffn_e2_l3"), ("e2d", "L0_ffn_e2_l4"),
                    ("e3a", "L0_ffn_e3_down"), ("e3b", "L0_ffn_e3_up"),
                    ("e3c", "L0_ffn_e3_out")]:
        m[nm] = np.ascontiguousarray(np.asarray(inp[key], f32).T)
    m["lnw"] = chunk_major(np.asarray(inp["L0_attn_ln_w"], f32))
    m["lnb"] = chunk_major(np.asarray(inp["L0_attn_ln_b"], f32))
    m["n1w"] = chunk_major(np.asarray(inp["L0_ffn_n1_w"], f32))
    m["n2w"] = chunk_major(np.asarray(inp["L0_ffn_n2_w"], f32))
    m["n2wtok"] = np.tile(np.asarray(inp["L0_ffn_n2_w"], f32).reshape(1, D), (128, 1))
    m["qnw"] = np.asarray(inp["L0_ffn_qn_w"], f32).reshape(HD, 1)
    m["knw"] = np.asarray(inp["L0_ffn_kn_w"], f32).reshape(HD, 1)
    cos_tm, sin_tm = rope_tables_tm(n_chunks)
    m["cos_tm"] = cos_tm
    m["sin_tm"] = sin_tm
    m["ident"] = np.eye(128, dtype=bf)
    m["identf"] = np.eye(128, dtype=f32)
    m["dmask"] = np.ascontiguousarray(parity_masks(parity).reshape(128, 256))
    m["triu"] = np.ascontiguousarray(triu_mask())
    m["onesrow"] = np.ones((1, N), dtype=bf)
    m["onescol"] = np.ones((128, 1), dtype=bf)
    m["baseoff"] = np.array([[parity * 128 * D]], dtype=np.uint32)
    m["consts"] = np.tile(np.array(
        [EPS_LN, EPS_RMS, 1.0 / D, 1.0 / HD, 1e-8, 64.0, 0.0, 0.0],
        dtype=f32), (128, 1))
    return m


def assemble_output(core_outs, n_chunks):
    S = n_chunks // 2
    out = np.zeros((4, n_chunks * 128, D), np.float32)
    for b in range(4):
        for p_ in range(2):
            co = core_outs[b * 2 + p_].reshape(S, 128, D)
            for i in range(S):
                out[b, (2 * i + p_) * 128:(2 * i + p_ + 1) * 128, :] = co[i]
    return out


def build_program(n_chunks, dbg=(), detect_races=True, upto=4, p1sub=9):
    S = n_chunks // 2
    N = n_chunks * 128
    NSUP = n_chunks // 4
    T_OWN = S * 128
    NT = -(-T_OWN // 512)
    dbg = set(dbg)

    nc = bass.Bass("TRN2", target_bir_lowering=False,
                   detect_race_conditions=detect_races)

    di = {}
    def inp(name, shape, dt=F32):
        di[name] = nc.dram_tensor(name, list(shape), dt, kind="ExternalInput")
        return di[name]

    x_hbm = inp("x", (N, D))
    for nm in ["wq1", "wk1", "wv1", "wo1"]:
        inp(nm, (D, D))
    inp("wg1", (D, HD)); inp("wg2", (HD, D))
    for nm in ["wq2", "wk2", "wv2", "wo2"]:
        inp(nm, (D, D))
    inp("gw", (D, 4))
    inp("e0u", (D, HID)); inp("e0d", (HID, D))
    inp("e1a", (D, HID)); inp("e1b", (D, HID)); inp("e1d", (HID, D))
    inp("e2a", (D, HID)); inp("e2b", (HID, HID)); inp("e2c", (HID, HID)); inp("e2d", (HID, D))
    inp("e3a", (D, D)); inp("e3b", (D, HID)); inp("e3c", (HID, D))
    for nm in ["lnw", "lnb", "n1w", "n2w"]:
        inp(nm, (128, NCH))
    inp("n2wtok", (128, D))
    inp("qnw", (HD, 1)); inp("knw", (HD, 1))
    inp("cos_tm", (128, n_chunks * HD), BF16)
    inp("sin_tm", (128, n_chunks * HD), BF16)
    inp("ident", (128, 128), BF16)
    inp("identf", (128, 128))
    inp("dmask", (128, 256), BF16)
    inp("triu", (128, 128), BF16)
    inp("onesrow", (1, N), BF16)
    inp("onescol", (128, 1), BF16)
    inp("baseoff", (1, 1), mybir.dt.uint32)
    inp("consts", (128, 8))

    out_hbm = nc.dram_tensor("out", [T_OWN, D], F32, kind="ExternalOutput")
    y1G = nc.dram_tensor("y1G", [N, D], F32)
    hbG = nc.dram_tensor("hbG", [N, D], BF16)
    kfG = nc.dram_tensor("kfG", [H, 65, N], BF16)
    vG = nc.dram_tensor("vG", [H, N, 65], BF16)
    x2G = nc.dram_tensor("x2G", [T_OWN, D], F32)
    hbOwnG = nc.dram_tensor("hbOwnG", [T_OWN, D], BF16)
    y1OwnG = nc.dram_tensor("y1OwnG", [T_OWN, D], F32)
    dbg_t = {}
    for nm in dbg:
        shp = {"y1": [N, D], "hb": [N, D], "q2": [T_OWN, D], "kf": [H, 65, N],
               "vg": [H, N, 65], "ao": [T_OWN, D], "x2": [T_OWN, D],
               "xf": [T_OWN, D], "wts": [T_OWN, 4], "moe": [T_OWN, D]}[nm]
        dbg_t[nm] = nc.dram_tensor("dbg_" + nm, shp, F32, kind="ExternalOutput")

    ctx = ExitStack()
    with ctx:
        p = Prog(nc)
        for s_ in ["BAR", "LDW", "LW0", "LW1", "LW2", "LW3", "LW4", "LW5", "LK0", "LK1", "LX0", "LX1", "SX0", "SX1", "LDS", "LDG", "ST", "PE", "VE", "SC", "GP"]:
            p.sem(s_, ctx)

        def mksb(c, name, shape, dt):
            return c.enter_context(nc.sbuf_tensor(name, list(shape), dt))

        sb = lambda name, shape, dt: mksb(ctx, name, shape, dt)
        ps = lambda name, shape, dt=F32: ctx.enter_context(nc.psum_tensor(name, list(shape), dt))

        ident = sb("identsb", (128, 128), BF16)
        identf = sb("identfsb", (128, 128), F32)
        consts = sb("constssb", (128, 8), F32)
        cos_tm = sb("costm", (128, n_chunks * HD), BF16)
        sin_tm = sb("sintm", (128, n_chunks * HD), BF16)
        dmask = sb("dmasksb", (128, 2, 128), BF16)
        triu = sb("triusb", (128, 128), BF16)
        onescol = sb("onescolsb", (128, 1), BF16)
        vecs = {nm: sb(nm + "sb", (128, NCH), F32) for nm in ["lnw", "n1w", "n2w"]}
        lnb_bf = sb("lnbbf", (128, NCH), BF16)
        qnw = sb("qnwsb", (HD, 1), F32)
        knw = sb("knwsb", (HD, 1), F32)
        qnrow = sb("qnrowsb", (1, HD), F32)
        knrow = sb("knrowsb", (1, HD), F32)
        mqk = sb("mqksb", (1, 2), F32)
        mh_sb = sb("mhsb", (1, 1), F32)
        negmh = sb("negmhsb", (1, 1), F32)
        qnkn = sb("qnknsb", (HD, 1), F32)
        c1row = sb("c1row", (1, D), BF16)
        c2row = sb("c2row", (1, D), BF16)
        stat = {nm: sb("st_" + nm, (128, 4), F32) for nm in
                ["sum", "sumsq", "mu", "mun", "var", "r", "sd", "r3", "tmp", "tmp2"]}
        stat16 = sb("stat16", (128, H), F32)
        stat64 = sb("stat64", (128, 64), F32)
        stat16b = sb("stat16b", (128, H), F32)
        rows = sb("rowstage", (1, 2, 512), BF16)
        bf_scr = sb("bfscr", (128, D), BF16)
        bo_sb = sb("bo_sb", (1, 1), mybir.dt.uint32)

        pA = ps("pA", (128, 512), F32)
        pB = ps("pB", (128, 512), F32)
        pTR = ps("pTR", (128, 1024), BF16)
        pC = ps("pC", (128, 512), F32)
        pD = ps("pD", (128, 512), F32)
        pE2 = ps("pE2", (128, 512), F32)
        pF = ps("pF", (128, 512), F32)
        pG = ps("pG", (128, 512), F32)
        avP = [pD, pE2, pF, pG]
        trp = BankPool(p, [0, 512], eager=True)
        mmp = BankPool(p, [pA, pB])

        for t, name in [(consts, "consts"), (qnw, "qnw"), (knw, "knw")]:
            p.dma("sync", t[:], di[name][:, :], "LDS")
        p.dma("sync", qnrow[:], di["qnw"][:, :].rearrange("a b -> b a"), "LDS")
        p.dma("sync", knrow[:], di["knw"][:, :].rearrange("a b -> b a"), "LDS")
        for nm, t in vecs.items():
            p.dma("sync", t[:], di[nm][:, :], "LDS")
        p.dma("sync", ident[:], di["ident"][:, :], "LDS")
        p.dma("sync", identf[:], di["identf"][:, :], "LDS")
        p.dma("sync", cos_tm[:], di["cos_tm"][:, :], "LDS")
        p.dma("sync", sin_tm[:], di["sin_tm"][:, :], "LDS")
        p.dma("sync", dmask[:], di["dmask"][:, :].rearrange("p (a b) -> p a b", a=2), "LDS")
        p.dma("sync", triu[:], di["triu"][:, :], "LDS")
        p.dma("sync", onescol[:], di["onescol"][:, :], "LDS")
        p.dma("sync", bo_sb[:], di["baseoff"][:, :], "LDS")
        p.dma("gpsimd", lnb_bf[:], di["lnb"][:, :], "LDG")
        ld_const = p.ctr["LDS"]
        ld_constg = p.ctr["LDG"]

        bo_reg = ctx.enter_context(nc.gpsimd.register("bo_reg"))
        off_reg = ctx.enter_context(nc.gpsimd.register("off_reg"))
        p.wait("gpsimd", "LDS", ld_const)
        p.emit("gpsimd", lambda g: g.reg_load(bo_reg, bo_sb[:1, :1]))

        def gather_all_own(dstT, srcT, sem="LDG"):
            def f(g, dstT=dstT, srcT=srcT):
                g.dma_start(dstT[:, :],
                            bass.AP(srcT, bo_reg, [[256 * D, S], [D, 128], [1, D]])
                            ).then_inc(p.sems[sem], 16)
            p.ctr[sem] += 16
            p.emit("gpsimd", f)
            return p.ctr[sem]

        def load_weight(dst, name, sem="LDW"):
            src = di[name][:, :].rearrange("(c p) o -> p c o", p=128)
            return p.dma("gpsimd", dst[:], src, sem)

        def tr_batch(srcs, evac_fn, idf=False):
            if len(srcs) > 4:
                trp.flush()
                oi, off = None, 0
            else:
                oi, off = trp.next()
            def tf(t, srcs=srcs, off=off):
                for j, src in enumerate(srcs):
                    r = t.transpose(pTR[0:src.shape[-1], off + j * 128:off + j * 128 + src.shape[0]],
                                    src, identf[:] if idf else ident[:])
                return r
            pe_t = p.op("tensor", lambda t, f=tf: f(t), "PE")
            p.wait("vector", "PE", pe_t)
            ve_t = p.op("vector", lambda v, f=evac_fn, off=off: f(v, off), "VE")
            if oi is None:
                p.wait("tensor", "VE", ve_t)
            else:
                trp.done(oi, "VE", ve_t)
            return ve_t

        def transpose_tm_to_fm(src_ap_fn, dst_fm, tc, n_feat_chunks=NCH):
            for cb in range(0, n_feat_chunks, 8):
                nblk = min(8, n_feat_chunks - cb)
                def ev(v, off, cb=cb, nblk=nblk, tc=tc):
                    return v.tensor_copy(
                        dst_fm[:, cb:cb + nblk, tc * 128:(tc + 1) * 128],
                        pTR[:, off:off + nblk * 128].rearrange("p (a b) -> p a b", a=nblk))
                tr_batch([src_ap_fn(cb + j) for j in range(nblk)], ev)

        def stats_ln(x_ap, idx, eps_col, extra_hb=False):
            p.wait("scalar", "VE", p.ctr["VE"])
            p.drain("vector")
            sc_t = p.op("scalar", lambda s, x_ap=x_ap, idx=idx: s.activation(
                bf_scr[:, 0:x_ap.shape[-1]], x_ap, AF.Square,
                accum_out=stat["sumsq"][:, idx:idx + 1]), "SC")
            ve_t = p.op("vector", lambda v, x_ap=x_ap, idx=idx: v.reduce_sum(
                stat["sum"][:, idx:idx + 1], x_ap, axis=AX.X), "VE")
            p.wait("vector", "SC", sc_t)
            i = idx
            nfree = x_ap.shape[-1]

            def vchain(v, i=i, nfree=nfree):
                v.drain()
                v.tensor_scalar_mul(stat["mu"][:, i:i + 1], stat["sum"][:, i:i + 1], 1.0 / nfree)
                v.tensor_scalar_mul(stat["tmp"][:, i:i + 1], stat["sumsq"][:, i:i + 1], 1.0 / nfree)
                v.drain()
                v.tensor_tensor(stat["tmp2"][:, i:i + 1], stat["mu"][:, i:i + 1],
                                stat["mu"][:, i:i + 1], ALU.mult)
                v.tensor_scalar_mul(stat["mun"][:, i:i + 1], stat["mu"][:, i:i + 1], -1.0)
                v.drain()
                v.tensor_tensor(stat["var"][:, i:i + 1], stat["tmp"][:, i:i + 1],
                                stat["tmp2"][:, i:i + 1], ALU.subtract)
                return v.drain()
            ve_t = p.op("vector", lambda v, f=vchain: f(v), "VE")
            p.wait("scalar", "VE", ve_t)
            sc_t = p.op("scalar", lambda s, i=i, eps_col=eps_col: s.activation(
                stat["sd"][:, i:i + 1], stat["var"][:, i:i + 1], AF.Sqrt,
                bias=eps_col), "SC")
            p.wait("vector", "SC", sc_t)

            def vchain2(v, i=i, extra=extra_hb):
                v.reciprocal(stat["r"][:, i:i + 1], stat["sd"][:, i:i + 1])
                if extra:
                    v.drain()
                    v.tensor_tensor(stat["tmp"][:, i:i + 1], stat["var"][:, i:i + 1],
                                    stat["r"][:, i:i + 1], ALU.mult)
                    v.drain()
                    v.tensor_tensor(stat["tmp"][:, i:i + 1], stat["tmp"][:, i:i + 1],
                                    stat["r"][:, i:i + 1], ALU.mult)
                return v.drain()
            ve_t = p.op("vector", lambda v, f=vchain2: f(v), "VE")
            if extra_hb:
                p.wait("scalar", "VE", ve_t)
                sc_t = p.op("scalar", lambda s, i=i: s.activation(
                    stat["tmp2"][:, i:i + 1], stat["tmp"][:, i:i + 1], AF.Sqrt,
                    bias=consts[:, 1:2]), "SC")
                p.wait("vector", "SC", sc_t)

                def vchain3(v, i=i):
                    v.reciprocal(stat["tmp"][:, i:i + 1], stat["tmp2"][:, i:i + 1])
                    v.drain()
                    v.tensor_tensor(stat["r3"][:, i:i + 1], stat["tmp"][:, i:i + 1],
                                    stat["r"][:, i:i + 1], ALU.mult)
                    return v.drain()
                p.op("vector", lambda v, f=vchain3: f(v), "VE")

        def rms_heads(t_ap, rdst, eps_col_idx=1):
            p.wait("scalar", "VE", p.ctr["VE"])
            p.drain("vector")
            sc_t = p.op("scalar", lambda s, t_ap=t_ap: s.activation(
                bf_scr[:, 0:D], t_ap, AF.Square), "SC")
            p.wait("vector", "SC", sc_t)

            def vch(v, rdst=rdst):
                v.tensor_reduce(stat16b[:], bf_scr[:, 0:D].rearrange(
                    "p (h d) -> p h d", h=H), AX.X, ALU.add)
                v.drain()
                v.tensor_scalar_mul(stat16b[:], stat16b[:], 1.0 / HD)
                return v.drain()
            ve_t = p.op("vector", lambda v, f=vch: f(v), "VE")
            p.wait("scalar", "VE", ve_t)
            sc_t = p.op("scalar", lambda s: s.activation(
                stat16b[:], stat16b[:], AF.Sqrt, bias=consts[:, eps_col_idx:eps_col_idx + 1]), "SC")
            p.wait("vector", "SC", sc_t)

            def vch2(v, rdst=rdst):
                v.reciprocal(rdst[:], stat16b[:])
                return v.drain()
            p.op("vector", lambda v, f=vch2: f(v), "VE")

        def apply_heads(t_ap3, rsrc):
            def vap(v, t_ap3=t_ap3, rsrc=rsrc):
                for h in range(H):
                    v.tensor_scalar_mul(t_ap3[:, h, :], t_ap3[:, h, :], rsrc[:, h:h + 1])
                return v.drain()
            return p.op("vector", lambda v, f=vap: f(v), "VE")

        eps_ln_col = consts[:, 0:1]

        p1ctx = ExitStack()
        sb1 = lambda name, shape, dt: mksb(p1ctx, name, shape, dt)
        w1 = {nm: sb1(nm + "s", (128, NCH, D), BF16) for nm in ["wq1", "wk1", "wv1", "wo1"]}
        wg1s = sb1("wg1s", (128, NCH, HD), BF16)
        wg2s = sb1("wg2s", (HD, D), BF16)
        xw = sb1("xw", (128, 2, 4, D), F32)
        xb = sb1("xb", (128, D), BF16)
        x_fm = sb1("x_fm", (128, NCH, 512), BF16)
        q1t = sb1("q1t", (128, 4, D), BF16)
        k1t = sb1("k1t", (128, 4, D), BF16)
        v1t = sb1("v1t", (128, 4, D), BF16)
        ro_tmp = xb
        q1f = sb1("q1f", (64, H, 512), BF16)
        k1f = sb1("k1f", (64, H, 512), BF16)
        gate_fm = sb1("gate_fm", (128, NCH, 512), BF16)
        z_fm = x_fm
        z_tm = sb1("z_tm", (128, D), BF16)
        g1sb = z_tm[0:HD, 0:512]
        S_sb = sb1("S_sb", (64, 2, 512), BF16)
        S_f32 = sb1("S_f32", (64, 2, 512), F32)
        e_sb = z_tm[:, 512:768]
        y1buf = sb1("y1buf", (128, D), F32)
        hbbuf = sb1("hbbuf", (128, D), BF16)
        pE = pC
        pO = pD
        pSa = pE2
        pSb = pF
        pMini = pG

        for nm in ["wq1", "wk1", "wv1", "wo1"]:
            load_weight(w1[nm], nm)
        load_weight(wg1s, "wg1")
        p.dma("gpsimd", wg2s[:], di["wg2"][:, :], "LDW")
        ldw_p1 = p.ctr["LDW"]

        p.wait("vector", "LDS", ld_const)
        p.op("vector", lambda v: v.tensor_tensor(qnkn[:], qnw[:], knw[:], ALU.mult), "VE")
        def mhv(v):
            v.tensor_tensor(qnrow[:], qnrow[:], qnrow[:], ALU.mult)
            v.drain()
            v.reduce_max(mqk[:, 0:1], qnrow[:], axis=AX.X)
            v.drain()
            v.tensor_tensor(knrow[:], knrow[:], knrow[:], ALU.mult)
            v.drain()
            v.reduce_max(mqk[:, 1:2], knrow[:], axis=AX.X)
            v.drain()
            v.tensor_tensor(negmh[:], mqk[:, 0:1], mqk[:, 1:2], ALU.mult)
            return v.drain()
        ve_mh = p.op("vector", lambda v, f=mhv: f(v), "VE")
        p.wait("scalar", "LDS", ld_const)
        p.wait("scalar", "VE", ve_mh)
        sc_mh = p.op("scalar", lambda s: s.activation(
            mh_sb[:], negmh[:], AF.Sqrt, scale=consts[0:1, 5:6]), "SC")
        p.wait("vector", "SC", sc_mh)
        ve_mh2 = p.op("vector", lambda v: v.tensor_scalar_mul(negmh[:], mh_sb[:], -1.0), "VE")
        p.wait("tensor", "LDW", ldw_p1)
        p.wait("tensor", "LDG", ld_constg)
        for ot in range(2):
            def mmres(t, ot=ot):
                for c in range(NCH):
                    r = t.matmul(pMini[0:1, :], lnb_bf[:, c:c + 1],
                                 w1["wo1"][:, c, ot * 512:(ot + 1) * 512],
                                 start=(c == 0), stop=(c == NCH - 1), skip_group_check=True)
                return r
            pe_t = p.op("tensor", lambda t, f=mmres: f(t), "PE")
            p.wait("vector", "PE", pe_t)
            ve_t = p.op("vector", lambda v, ot=ot: v.tensor_copy(
                c2row[:, ot * 512:(ot + 1) * 512], pMini[0:1, :]), "VE")
            p.wait("tensor", "VE", ve_t)
        p.wait("vector", "LDW", ldw_p1)
        for c in range(NCH):
            p.op("vector", lambda v, c=c: v.tensor_scalar_mul(
                w1["wo1"][:, c, :], w1["wo1"][:, c, :], vecs["lnw"][:, c:c + 1]), "VE")
        p.drain("vector")
        fold_done = p.inc("vector", "VE")
        p.wait("tensor", "VE", fold_done)
        for ot in range(2):
            def mmc1(t, ot=ot):
                for c in range(NCH):
                    r = t.matmul(pMini[0:1, :], onescol[:],
                                 w1["wo1"][:, c, ot * 512:(ot + 1) * 512],
                                 start=(c == 0), stop=(c == NCH - 1), skip_group_check=True)
                return r
            pe_t = p.op("tensor", lambda t, f=mmc1: f(t), "PE")
            p.wait("vector", "PE", pe_t)
            ve_t = p.op("vector", lambda v, ot=ot: v.tensor_copy(
                c1row[:, ot * 512:(ot + 1) * 512], pMini[0:1, :]), "VE")
            p.wait("tensor", "VE", ve_t)

        if upto <= 0:
            p.wait("vector", "VE", p.ctr["VE"])
            p.op("gpsimd", lambda g: g.memset(y1buf[:], 0.0), "GP")
            p.wait("sync", "GP", p.ctr["GP"])
            for _i in range(S):
                p.dma("sync", out_hbm[_i * 128:(_i + 1) * 128, :], y1buf[:], "ST")
            p.wait("sync", "ST", p.ctr["ST"])
            with nc.Block() as block:
                p.finalize(block)
            p1ctx.close()
            return nc

        p.barrier()

        p.op("gpsimd", lambda g: g.memset(S_sb[:], 0.0), "GP")
        p.op("gpsimd", lambda g: g.memset(S_f32[:], 0.0), "GP")
        p.wait("scalar", "GP", p.ctr["GP"])
        p.wait("tensor", "GP", p.ctr["GP"])
        p.wait("vector", "GP", p.ctr["GP"])

        xsrc = x_hbm[:, :].rearrange("(s c p) d -> s p c d", p=128, c=4)
        lxs = lambda i: "LX0" if i % 2 == 0 else "LX1"
        xw_ld = {0: p.dma("sync", xw[:, 0, :, :], xsrc[0], lxs(0))}
        for G in range(NSUP):
            xb_ = G % 2
            prev_z_ve = [None]
            if G + 1 < NSUP:
                xw_ld[G + 1] = p.dma("sync", xw[:, (G + 1) % 2, :, :], xsrc[G + 1], lxs(G + 1))
            p.wait("vector", lxs(G), xw_ld[G])
            for tc in range(4):
                cast_t = p.op("vector", lambda v, tc=tc, xb_=xb_: v.tensor_copy(xb[:], xw[:, xb_, tc, :]), "VE")
                p.wait("tensor", "VE", cast_t)
                transpose_tm_to_fm(lambda c: xb[:, c * 128:(c + 1) * 128], x_fm, tc)
            trp.flush()
            for tc in range(4 if p1sub >= 2 else 0):
                for wname, dstt, act in [("wq1", q1t, AF.Silu), ("wk1", k1t, AF.Silu),
                                         ("wv1", v1t, None)]:
                    for ot in range(2):
                        oi, bank = mmp.next()
                        def mm(t, wname=wname, tc=tc, ot=ot, bank=bank):
                            for c in range(NCH):
                                r = t.matmul(bank[:], x_fm[:, c, tc * 128:(tc + 1) * 128],
                                             w1[wname][:, c, ot * 512:(ot + 1) * 512],
                                             start=(c == 0), stop=(c == NCH - 1),
                                             skip_group_check=True)
                            return r
                        pe_t = p.op("tensor", lambda t, f=mm: f(t), "PE")
                        if act is not None:
                            p.wait("scalar", "PE", pe_t)
                            ev_t = p.op("scalar", lambda s, dstt=dstt, tc=tc, ot=ot, bank=bank, act=act:
                                        s.activation(dstt[:, tc, ot * 512:(ot + 1) * 512], bank[:], act), "SC")
                            mmp.done(oi, "SC", ev_t)
                        else:
                            p.wait("vector", "PE", pe_t)
                            ev_t = p.op("vector", lambda v, dstt=dstt, tc=tc, ot=ot, bank=bank:
                                        v.tensor_copy(dstt[:, tc, ot * 512:(ot + 1) * 512], bank[:]), "VE")
                            mmp.done(oi, "VE", ev_t)
            sc_proj = p.ctr["SC"]
            if p1sub < 3:
                continue
            oi, bank = mmp.next()
            def mmg1(t, bank=bank):
                for c in range(NCH):
                    r = t.matmul(bank[0:HD, :], wg1s[:, c, :], x_fm[:, c, :],
                                 start=(c == 0), stop=(c == NCH - 1), skip_group_check=True)
                return r
            pe_t = p.op("tensor", lambda t, f=mmg1: f(t), "PE")
            p.wait("vector", "PE", pe_t)
            ve_t = p.op("vector", lambda v, bank=bank: v.tensor_copy(g1sb[:], bank[0:HD, :]), "VE")
            mmp.done(oi, "VE", ve_t)
            p.wait("tensor", "VE", ve_t)
            for oc in range(NCH):
                oi, bank = mmp.next()
                def mmg2(t, oc=oc, bank=bank):
                    return t.matmul(bank[:], wg2s[:, oc * 128:(oc + 1) * 128], g1sb[:],
                                    start=True, stop=True, skip_group_check=True)
                pe_t = p.op("tensor", lambda t, f=mmg2: f(t), "PE")
                p.wait("scalar", "PE", pe_t)
                sc_t = p.op("scalar", lambda s, oc=oc, bank=bank: s.activation(
                    gate_fm[:, oc, :], bank[:, 0:512], AF.Sigmoid), "SC")
                mmp.done(oi, "SC", sc_t)
            sc_gate = p.ctr["SC"]
            if p1sub < 4:
                continue
            p.wait("vector", "SC", sc_proj)
            for tc in range(4):
                g = G * 4 + tc
                for tt in (q1t, k1t):
                    def ropef(v, tt=tt, tc=tc, g=g):
                        src3 = tt[:, tc, :].rearrange("p (h d) -> p h d", h=H)
                        tmp3 = ro_tmp[:].rearrange("p (h d) -> p h d", h=H)
                        sin3lo = sin_tm[:, g * HD:g * HD + 32].unsqueeze(1).broadcast_to([128, H, 32])
                        sin3hi = sin_tm[:, g * HD + 32:g * HD + 64].unsqueeze(1).broadcast_to([128, H, 32])
                        cos3 = cos_tm[:, g * HD:(g + 1) * HD].unsqueeze(1).broadcast_to([128, H, HD])
                        v.tensor_tensor(tmp3[:, :, 0:32], src3[:, :, 32:64], sin3lo, ALU.mult)
                        v.tensor_tensor(tmp3[:, :, 32:64], src3[:, :, 0:32], sin3hi, ALU.mult)
                        v.drain()
                        v.tensor_tensor(src3, src3, cos3, ALU.mult)
                        v.drain()
                        v.tensor_tensor(tt[:, tc, :], tt[:, tc, :], ro_tmp[:], ALU.add)
                        return v.drain()
                    p.op("vector", lambda v, f=ropef: f(v), "VE")
            rope_done = p.inc("vector", "VE")
            p.wait("tensor", "VE", rope_done)
            if p1sub < 5:
                continue
            for tc in range(4):
                for src_t, dst_f in [(q1t, q1f), (k1t, k1f)]:
                    for hb_ in range(0, H, 8):
                        def evh(v, off, dst_f=dst_f, tc=tc, hb_=hb_):
                            return v.tensor_copy(
                                dst_f[0:64, hb_:hb_ + 8, tc * 128:(tc + 1) * 128],
                                pTR[0:64, off:off + 1024].rearrange("p (a b) -> p a b", a=8))
                        tr_batch([src_t[:, tc, (hb_ + j) * 64:(hb_ + j + 1) * 64]
                                  for j in range(8)], evh)
            if p1sub < 6:
                continue
            trp.flush()
            p.wait("vector", "SC", sc_gate)

            def attn_tc(tc):
                for pp in range(NCH):
                    ha, hb2 = 2 * pp, 2 * pp + 1
                    def mme(t, tc=tc, ha=ha, hb2=hb2):
                        t.matmul(pE[:, 0:128], k1f[0:64, ha, tc * 128:(tc + 1) * 128],
                                 q1f[0:64, ha, tc * 128:(tc + 1) * 128], start=True, stop=False,
                                 skip_group_check=True)
                        return t.matmul(pE[:, 128:256], k1f[0:64, hb2, tc * 128:(tc + 1) * 128],
                                        q1f[0:64, hb2, tc * 128:(tc + 1) * 128], start=False, stop=True,
                                        skip_group_check=True)
                    pe_t = p.op("tensor", lambda t, f=mme: f(t), "PE")
                    p.wait("vector", "PE", pe_t)
                    ve_t = p.op("vector", lambda v: v.tensor_tensor(
                        e_sb[:].rearrange("p (a b) -> p a b", a=2),
                        pE[:, 0:256].rearrange("p (a b) -> p a b", a=2),
                        triu[:].unsqueeze(1).broadcast_to([128, 2, 128]), ALU.mult), "VE")
                    p.wait("tensor", "VE", ve_t)
                    if prev_z_ve[0] is not None:
                        p.wait("tensor", "VE", prev_z_ve[0])
                        prev_z_ve[0] = None
                    def mmav(t, tc=tc, pp=pp, ha=ha, hb2=hb2):
                        t.matmul(pO[0:64, 0:128], v1t[:, tc, ha * 64:(ha + 1) * 64],
                                 e_sb[:, 0:128], start=True, stop=False, skip_group_check=True)
                        t.matmul(pO[64:128, 0:128], v1t[:, tc, hb2 * 64:(hb2 + 1) * 64],
                                 e_sb[:, 128:256], start=True, stop=False,
                                 tile_position=(0, 64), skip_group_check=True)
                        t.matmul(pO[0:64, 0:128], S_sb[0:64, 0, pp * 64:(pp + 1) * 64],
                                 q1f[0:64, ha, tc * 128:(tc + 1) * 128], start=False, stop=True,
                                 skip_group_check=True)
                        t.matmul(pO[64:128, 0:128], S_sb[0:64, 1, pp * 64:(pp + 1) * 64],
                                 q1f[0:64, hb2, tc * 128:(tc + 1) * 128], start=False, stop=True,
                                 tile_position=(0, 64), skip_group_check=True)
                        t.matmul(pSa[0:64, pp * 64:(pp + 1) * 64],
                                 k1t[:, tc, ha * 64:(ha + 1) * 64],
                                 v1t[:, tc, ha * 64:(ha + 1) * 64],
                                 start=(pp == 0), stop=(pp == 7), skip_group_check=True)
                        return t.matmul(pSb[0:64, pp * 64:(pp + 1) * 64],
                                        k1t[:, tc, hb2 * 64:(hb2 + 1) * 64],
                                        v1t[:, tc, hb2 * 64:(hb2 + 1) * 64],
                                        start=(pp == 0), stop=(pp == 7), skip_group_check=True)
                    pe_t = p.op("tensor", lambda t, f=mmav: f(t), "PE")
                    p.wait("vector", "PE", pe_t)
                    ve_t = p.op("vector", lambda v, tc=tc, pp=pp: v.tensor_tensor(
                        z_fm[:, pp, tc * 128:(tc + 1) * 128], pO[:, 0:128],
                        gate_fm[:, pp, tc * 128:(tc + 1) * 128], ALU.mult), "VE")
                    prev_z_ve[0] = ve_t
                def supd(v):
                    v.tensor_tensor(S_f32[0:64, 0, :], S_f32[0:64, 0, :], pSa[0:64, :], ALU.add)
                    v.tensor_tensor(S_f32[0:64, 1, :], S_f32[0:64, 1, :], pSb[0:64, :], ALU.add)
                    v.drain()
                    v.tensor_copy(S_sb[:], S_f32[:])
                    return v.drain()
                ve_t = p.op("vector", lambda v, f=supd: f(v), "VE")
                p.wait("tensor", "VE", ve_t)

            def tail_tc(tc):
                g = G * 4 + tc
                def evz(v, off):
                    return v.tensor_copy(z_tm[:, 0:1024], pTR[:, 0:1024])
                tr_batch([z_fm[:, j, tc * 128:(tc + 1) * 128] for j in range(NCH)], evz)
                stats_ln(z_tm[:, :], 0, eps_ln_col)
                p.wait("tensor", "VE", p.ctr["VE"])
                def mini2(t):
                    t.transpose(pMini[0:1, 0:128], stat["mun"][:, 0:1], identf[:])
                    return t.transpose(pMini[0:1, 128:256], stat["sd"][:, 0:1], identf[:])
                pe_t = p.op("tensor", lambda t, f=mini2: f(t), "PE")
                p.wait("vector", "PE", pe_t)
                ve_t = p.op("vector", lambda v, tc=tc: v.tensor_copy(
                    rows[0:1, 0:2, tc * 128:(tc + 1) * 128],
                    pMini[0:1, 0:256].rearrange("p (a b) -> p a b", a=2)), "VE")
                p.wait("tensor", "VE", ve_t)
                r_col = stat["r"][:, 0:1]
                for ot in range(2):
                    oi, bank = mmp.next()
                    def mmwo(t, tc=tc, ot=ot, bank=bank):
                        for c in range(NCH):
                            t.matmul(bank[:], z_fm[:, c, tc * 128:(tc + 1) * 128],
                                     w1["wo1"][:, c, ot * 512:(ot + 1) * 512],
                                     start=(c == 0), stop=False, skip_group_check=True)
                        t.matmul(bank[:], rows[0:1, 0, tc * 128:(tc + 1) * 128],
                                 c1row[0:1, ot * 512:(ot + 1) * 512], start=False, stop=False,
                                 skip_group_check=True)
                        return t.matmul(bank[:], rows[0:1, 1, tc * 128:(tc + 1) * 128],
                                        c2row[0:1, ot * 512:(ot + 1) * 512], start=False, stop=True,
                                        skip_group_check=True)
                    pe_t = p.op("tensor", lambda t, f=mmwo: f(t), "PE")
                    p.wait("vector", "PE", pe_t)
                    ve_t = p.op("vector", lambda v, tc=tc, ot=ot, bank=bank, r_col=r_col, xb_=xb_:
                                v.scalar_tensor_tensor(
                                    y1buf[:, ot * 512:(ot + 1) * 512], bank[:], r_col,
                                    xw[:, xb_, tc, ot * 512:(ot + 1) * 512], ALU.mult, ALU.add), "VE")
                    mmp.done(oi, "VE", ve_t)
                stats_ln(y1buf[:, :], 1, eps_ln_col, extra_hb=True)
                def vfin(v):
                    v.tensor_scalar(hbbuf[:], y1buf[:, :], stat["mun"][:, 1:2],
                                    stat["r3"][:, 1:2], ALU.add, ALU.mult)
                    v.drain()
                    v.tensor_scalar(y1buf[:], y1buf[:, :], stat["mun"][:, 1:2],
                                    stat["r"][:, 1:2], ALU.add, ALU.mult)
                    return v.drain()
                ve_t = p.op("vector", lambda v, f=vfin: f(v), "VE")
                p.wait("sync", "VE", ve_t)
                p.dma("sync", y1G[g * 128:(g + 1) * 128, :], y1buf[:], "ST")
                st2 = p.dma("sync", hbG[g * 128:(g + 1) * 128, :], hbbuf[:], "ST")
                p.wait("vector", "ST", st2)

            for tc in range(4):
                attn_tc(tc)
                if p1sub >= 7 and tc >= 1:
                    tail_tc(tc - 1)
            if p1sub >= 7:
                tail_tc(3)

        if "y1" in dbg and p1sub >= 7:
            p.wait("sync", "ST", p.ctr["ST"])
            for g in range(n_chunks):
                ld = p.dma("sync", y1buf[:], y1G[g * 128:(g + 1) * 128, :], "LDS")
                p.wait("sync", "LDS", ld)
                p.dma("sync", dbg_t["y1"][g * 128:(g + 1) * 128, :], y1buf[:], "ST")
                p.wait("sync", "ST", p.ctr["ST"])

        p.barrier()
        p.wait("sync", "ST", p.ctr["ST"])
        p.wait("gpsimd", "ST", p.ctr["ST"])
        gather_all_own(hbOwnG, hbG)
        gather_all_own(y1OwnG, y1G)
        own_gathers = p.ctr["LDG"]
        if upto <= 1:
            p.wait("gpsimd", "LDG", own_gathers)
            p.op("gpsimd", lambda g: g.memset(y1buf[:], 0.0), "GP")
            p.wait("sync", "GP", p.ctr["GP"])
            for _i in range(S):
                p.dma("sync", out_hbm[_i * 128:(_i + 1) * 128, :], y1buf[:], "ST")
            p.wait("sync", "ST", p.ctr["ST"])
            with nc.Block() as block:
                p.finalize(block)
            p1ctx.close()
            return nc
        p1ctx.close()

        p2ctx = ExitStack()
        sb2 = lambda name, shape, dt: mksb(p2ctx, name, shape, dt)
        unionA = ctx.enter_context(nc.sbuf_tensor("unionA", [128, S * D], BF16, side="right"))
        q2n = unionA[:].rearrange("p (s d) -> p s d", s=S)
        wts_t = ctx.enter_context(nc.sbuf_tensor("wtsT", [128, S, 4], F32, side="right"))
        wts = wts_t[:]
        attnctx = ExitStack()
        attnout_t = attnctx.enter_context(nc.sbuf_tensor("attnoutT", [128, S * D], BF16, side="right"))
        attnout = attnout_t[:].rearrange("p (s d) -> p s d", s=S)
        w2 = {nm: sb2(nm + "s", (128, NCH, D), BF16) for nm in ["wq2", "wk2", "wv2"]}
        hbt = sb2("hbt", (128, 2, 4, D), BF16)
        hb_fm = sb2("hb_fm", (128, NCH, 512), BF16)
        k2t = sb2("k2t", (128, 4, D), BF16)
        vaug = sb2("vaug", (128, 4, H, 65), BF16)
        kfstage = sb2("kfstage", (64, H, 512), BF16)
        hbq = sb2("hbq", (128, 2, D), BF16)

        for nm in ["wq2", "wk2", "wv2"]:
            load_weight(w2[nm], nm)
        ldw_p2 = p.ctr["LDW"]
        p.wait("vector", "LDW", ldw_p2)
        for nm in ["wq2", "wk2", "wv2"]:
            for c in range(NCH):
                p.op("vector", lambda v, nm=nm, c=c: v.tensor_scalar_mul(
                    w2[nm][:, c, :], w2[nm][:, c, :], vecs["n1w"][:, c:c + 1]), "VE")
        p.drain("vector")
        foldw2 = p.inc("vector", "VE")
        p.wait("tensor", "VE", foldw2)
        for h in range(H):
            p.dma("sync", kfG[h, 64:65, :], di["onesrow"][:, :], "ST")
        gp_t = p.op("gpsimd", lambda g: g.memset(vaug[:, :, :, 64:65], 1.0), "GP")
        p.wait("vector", "GP", gp_t)
        p.wait("sync", "ST", p.ctr["ST"])

        hbsrc = hbG[:, :].rearrange("(s c p) d -> s p c d", p=128, c=4)
        hb_ld = {0: p.dma("sync", hbt[:, 0, :, :], hbsrc[0], lxs(0))}
        tr_mark = {}
        st_mark = {}
        sq_scr = hb_fm[:].rearrange("p a b -> p (a b)")
        k2t_flat = k2t[:].rearrange("p a b -> p (a b)")
        k2tv = k2t[:].rearrange("p a (h d) -> p d (a h)", d=HD)
        sqv = sq_scr.rearrange("p (x d) -> p x d", d=HD)
        s64b = stat64[:].unsqueeze(1).broadcast_to([128, HD, 4 * H])
        for G in range(NSUP):
            hbuf = G % 2
            if G + 1 < NSUP:
                if G - 1 in tr_mark:
                    p.wait("sync", "PE", tr_mark[G - 1])
                hb_ld[G + 1] = p.dma("sync", hbt[:, (G + 1) % 2, :, :], hbsrc[G + 1], lxs(G + 1))
            p.wait("tensor", lxs(G), hb_ld[G])
            for tc in range(4):
                transpose_tm_to_fm(lambda c, tc=tc, hbuf=hbuf:
                                   hbt[:, hbuf, tc, c * 128:(c + 1) * 128], hb_fm, tc)
            tr_mark[G] = p.ctr["PE"]
            trp.flush()
            if G - 1 in st_mark:
                p.wait("vector", "ST", st_mark[G - 1])
            for tc in range(4):
                for ot in range(2):
                    oi, bank = mmp.next()
                    def mmk(t, tc=tc, ot=ot, bank=bank):
                        for c in range(NCH):
                            r = t.matmul(bank[:], hb_fm[:, c, tc * 128:(tc + 1) * 128],
                                         w2["wk2"][:, c, ot * 512:(ot + 1) * 512],
                                         start=(c == 0), stop=(c == NCH - 1), skip_group_check=True)
                        return r
                    pe_t = p.op("tensor", lambda t, f=mmk: f(t), "PE")
                    p.wait("vector", "PE", pe_t)
                    ve_t = p.op("vector", lambda v, tc=tc, ot=ot, bank=bank:
                                v.tensor_copy(k2t[:, tc, ot * 512:(ot + 1) * 512], bank[:]), "VE")
                    mmp.done(oi, "VE", ve_t)
                for ot in range(2):
                    oi, bank = mmp.next()
                    def mmv(t, tc=tc, ot=ot, bank=bank):
                        for c in range(NCH):
                            r = t.matmul(bank[:], hb_fm[:, c, tc * 128:(tc + 1) * 128],
                                         w2["wv2"][:, c, ot * 512:(ot + 1) * 512],
                                         start=(c == 0), stop=(c == NCH - 1), skip_group_check=True)
                        return r
                    pe_t = p.op("tensor", lambda t, f=mmv: f(t), "PE")
                    p.wait("vector", "PE", pe_t)
                    ve_t = p.op("vector", lambda v, tc=tc, ot=ot, bank=bank:
                                v.tensor_copy(
                                    vaug[:, tc, ot * 8:(ot + 1) * 8, 0:64],
                                    bank[:, 0:512].rearrange("p (a b) -> p a b", a=8)), "VE")
                    mmp.done(oi, "VE", ve_t)
            p.wait("scalar", "VE", p.ctr["VE"])
            p.drain("vector")
            sc_t = p.op("scalar", lambda s: s.activation(
                sq_scr, k2t_flat, AF.Square), "SC")
            p.wait("vector", "SC", sc_t)
            def vrms(v):
                v.tensor_reduce(stat64[:], sqv, AX.X, ALU.add)
                v.drain()
                v.tensor_scalar_mul(stat64[:], stat64[:], 1.0 / HD)
                return v.drain()
            ve_t = p.op("vector", lambda v, f=vrms: f(v), "VE")
            p.wait("scalar", "VE", ve_t)
            sc_t = p.op("scalar", lambda s: s.activation(
                stat64[:], stat64[:], AF.Sqrt, bias=consts[:, 1:2]), "SC")
            p.wait("vector", "SC", sc_t)
            def vrms2(v):
                v.reciprocal(stat64[:], stat64[:])
                v.drain()
                v.tensor_tensor(k2tv, k2tv, s64b, ALU.mult)
                return v.drain()
            ve_t = p.op("vector", lambda v, f=vrms2: f(v), "VE")
            p.wait("tensor", "VE", ve_t)
            for tc in range(4):
                for hb_ in range(0, H, 8):
                    def evk(v, off, tc=tc, hb_=hb_):
                        return v.tensor_scalar_mul(
                            kfstage[0:64, hb_:hb_ + 8, tc * 128:(tc + 1) * 128],
                            pTR[0:64, off:off + 1024].rearrange("p (a b) -> p a b", a=8),
                            qnkn[0:64, 0:1])
                    tr_batch([k2t[:, tc, (hb_ + j) * 64:(hb_ + j + 1) * 64]
                              for j in range(8)], evk)
            p.wait("sync", "VE", p.ctr["VE"])
            p.dma("sync", kfG[:, :, :].rearrange("h p n -> p h n")[0:64, :, G * 512:(G + 1) * 512],
                  kfstage[:], "ST")
            for tc in range(4):
                st_mark[G] = p.dma(
                    "sync",
                    vG[:, G * 512 + tc * 128:G * 512 + (tc + 1) * 128, :].rearrange(
                        "h p a -> p h a"),
                    vaug[:, tc, :, :], "ST")

        p.wait("sync", "LDG", own_gathers)
        q_ld = {0: p.dma("sync", hbq[:, 0, :], hbOwnG[0:128, :], lxs(0))}
        q_tr = {}
        for i in range(S):
            qb = i % 2
            if i + 1 < S:
                if i - 1 in q_tr:
                    p.wait("sync", "PE", q_tr[i - 1])
                q_ld[i + 1] = p.dma("sync", hbq[:, (i + 1) % 2, :],
                                    hbOwnG[(i + 1) * 128:(i + 2) * 128, :], lxs(i + 1))
            p.wait("tensor", lxs(i), q_ld[i])
            def evq(v, off):
                return v.tensor_copy(
                    hb_fm[:, 0:NCH, 0:128],
                    pTR[:, 0:1024].rearrange("p (a b) -> p a b", a=NCH))
            tr_batch([hbq[:, qb, j * 128:(j + 1) * 128] for j in range(NCH)], evq)
            q_tr[i] = p.ctr["PE"]
            for ot in range(2):
                oi, bank = mmp.next()
                def mmq2(t, ot=ot, bank=bank):
                    for c in range(NCH):
                        r = t.matmul(bank[:], hb_fm[:, c, 0:128],
                                     w2["wq2"][:, c, ot * 512:(ot + 1) * 512],
                                     start=(c == 0), stop=(c == NCH - 1), skip_group_check=True)
                    return r
                pe_t = p.op("tensor", lambda t, f=mmq2: f(t), "PE")
                p.wait("vector", "PE", pe_t)
                ve_t = p.op("vector", lambda v, i=i, ot=ot, bank=bank: v.tensor_copy(
                    q2n[:, i, ot * 512:(ot + 1) * 512], bank[:]), "VE")
                mmp.done(oi, "VE", ve_t)
        for g in range(0, S, 4):
            n = min(4, S - g)
            qflat = unionA[:, g * D:(g + n) * D]
            qv = qflat.rearrange("p (x d) -> p d x", d=HD)
            sqq = sq_scr[:, 0:n * D].rearrange("p (x d) -> p x d", d=HD)
            snb = stat64[:, 0:n * H].unsqueeze(1).broadcast_to([128, HD, n * H])
            p.wait("scalar", "VE", p.ctr["VE"])
            p.drain("vector")
            sc_t = p.op("scalar", lambda s, qflat=qflat, n=n: s.activation(
                sq_scr[:, 0:n * D], qflat, AF.Square), "SC")
            p.wait("vector", "SC", sc_t)
            def vrq(v, sqq=sqq, n=n):
                v.tensor_reduce(stat64[:, 0:n * H], sqq, AX.X, ALU.add)
                v.drain()
                v.tensor_scalar_mul(stat64[:, 0:n * H], stat64[:, 0:n * H], 1.0 / HD)
                return v.drain()
            ve_t = p.op("vector", lambda v, f=vrq: f(v), "VE")
            p.wait("scalar", "VE", ve_t)
            sc_t = p.op("scalar", lambda s, n=n: s.activation(
                stat64[:, 0:n * H], stat64[:, 0:n * H], AF.Sqrt, bias=consts[:, 1:2]), "SC")
            p.wait("vector", "SC", sc_t)
            def vrq2(v, qv=qv, snb=snb, n=n):
                v.reciprocal(stat64[:, 0:n * H], stat64[:, 0:n * H])
                v.drain()
                v.tensor_tensor(qv, qv, snb, ALU.mult)
                return v.drain()
            p.op("vector", lambda v, f=vrq2: f(v), "VE")

        if upto <= 2:
            p.barrier()
            p.wait("sync", "ST", p.ctr["ST"])
            p.op("gpsimd", lambda g: g.memset(hbq[:, 0, :], 0.0), "GP")
            p.wait("sync", "GP", p.ctr["GP"])
            for _i in range(S):
                p.dma("sync", out_hbm[_i * 128:(_i + 1) * 128, 0:512],
                      hbq[:, 0, :].bitcast(F32), "ST")
            p.wait("sync", "ST", p.ctr["ST"])
            with nc.Block() as block:
                p.finalize(block)
            p2ctx.close()
            return nc
        p.wait("sync", "ST", p.ctr["ST"])
        p.barrier()
        p2ctx.close()

        p3ctx = ExitStack()
        sb3 = lambda name, shape, dt: mksb(p3ctx, name, shape, dt)
        KF = [sb3(f"KF{j}", (65, N), BF16) for j in range(2)]
        Vh = [sb3(f"Vh{j}", (128, n_chunks, 65), BF16) for j in range(2)]
        Qa = sb3("Qa", (65, T_OWN), BF16)
        Pbuf = sb3("Pbuf", (128, T_OWN), BF16)
        avsb = sb3("avsb", (65, 512), BF16)
        mxs = sb3("mxs", (128, 8), F32)
        negm = sb3("negm", (128, 1), F32)
        rsp = sb3("rsp", (128, 4), F32)

        def i0_of(j):
            return j // 2

        p3p = BankPool(p, [pA, pB, pC])
        pe_consumed = {}
        lds_mark = {}

        def issue_head_loads(h):
            sem = "LK0" if h % 2 == 0 else "LK1"
            if h >= 2:
                p.wait("sync", "PE", pe_consumed[h - 2])
            p.dma("sync", KF[h % 2][:], kfG[h, :, :], sem)
            lds_mark[h] = (sem, p.dma("sync", Vh[h % 2][:],
                                vG[h, :, :].rearrange("(c p) a -> p c a", p=128), sem))

        p.op("vector", lambda v: v.tensor_copy(
            Qa[64:65, 0:T_OWN].rearrange("p (t o) -> p t o", o=1),
            negmh[0:1, 0:1].unsqueeze(1).broadcast_to([1, T_OWN, 1])), "VE")
        p.drain("vector")

        issue_head_loads(0)
        for h in range(H):
            buf = h % 2
            if h + 1 < H:
                issue_head_loads(h + 1)
            p.wait("tensor", lds_mark[h][0], lds_mark[h][1])
            p.wait("scalar", lds_mark[h][0], lds_mark[h][1])
            p.wait("vector", lds_mark[h][0], lds_mark[h][1])
            for i in range(0, S, 4):
                nsl = min(4, S - i)
                def evqa(v, off, i=i, nsl=nsl):
                    return v.tensor_scalar_mul(
                        Qa[0:64, i * 128:(i + nsl) * 128],
                        pTR[0:64, off:off + nsl * 128], 0.125)
                tr_batch([q2n[:, i + j, h * 64:(h + 1) * 64] for j in range(nsl)], evqa)
            trp.flush()
            p.wait("tensor", "VE", p.ctr["VE"])
            av_started = [False] * NT
            prev_av = None

            def evac_bank(t_i, h=h):
                cols = min(512, T_OWN - 512 * t_i)
                p.wait("vector", "PE", p.ctr["PE"])
                ve_t = p.op("vector", lambda v, t_i=t_i, cols=cols: v.tensor_copy(
                    avsb[:, 0:cols], avP[t_i][0:65, 0:cols]), "VE")
                p.drain("vector")
                p.wait("tensor", "VE", ve_t)
                nblk = -(-cols // 128)
                oi, bank = p3p.next()
                def trdiv_pe(t, cols=cols, nblk=nblk, bank=bank):
                    for blk in range(nblk):
                        nq = min(128, cols - blk * 128)
                        r = t.transpose(bank[:].bitcast(BF16)[0:nq, blk * 256:blk * 256 + 65],
                                        avsb[:, blk * 128:blk * 128 + nq],
                                        ident[0:65, 0:65])
                    return r
                pe_t = p.op("tensor", lambda t, f=trdiv_pe: f(t), "PE")
                p.wait("vector", "PE", pe_t)
                def vdiv(v, t_i=t_i, h=h, nblk=nblk, bank=bank):
                    bv = bank[:].bitcast(BF16)
                    den = bv.rearrange("p (s f) -> p s f", f=256)[:, 0:nblk, 64:65]
                    v.reciprocal(rsp[:, 0:nblk], den.rearrange("p s f -> p (s f)"))
                    v.drain()
                    pvT = bv.rearrange("p (s f) -> p f s", f=256)[:, 0:64, 0:nblk]
                    aoT = attnout[:, 4 * t_i:4 * t_i + nblk, h * 64:(h + 1) * 64].rearrange(
                        "p s f -> p f s")
                    rb = rsp[:, 0:nblk].unsqueeze(1).broadcast_to([128, 64, nblk])
                    v.tensor_tensor(aoT, pvT, rb, ALU.mult)
                    return v.drain()
                ve_t = p.op("vector", lambda v, f=vdiv: f(v), "VE")
                p3p.done(oi, "VE", ve_t)

            def emit_prev_av(prev):
                (jp, ap_, bp_, sem_p, val_p) = prev
                t_i = ap_ // 512
                st_flag = not av_started[t_i]
                av_started[t_i] = True
                last_j = min(8 * t_i + 7, n_chunks - 1)
                p.wait("tensor", sem_p, val_p)
                def mmav2(t, jp=jp, ap_=ap_, bp_=bp_, t_i=t_i, st_flag=st_flag,
                          last_j=last_j, buf=buf):
                    return t.matmul(avP[t_i][0:65, (ap_ - 512 * t_i):(bp_ - 512 * t_i)],
                                    Vh[buf][:, jp, :], Pbuf[:, ap_:bp_],
                                    start=st_flag, stop=(jp == last_j),
                                    skip_group_check=True)
                p.op("tensor", lambda t, f=mmav2: f(t), "PE")
                if jp == last_j:
                    evac_bank(t_i)

            for j in range(n_chunks):
                q0 = 128 * i0_of(j)
                a = q0
                while a < T_OWN:
                    b_ = min((a // 512 + 1) * 512, T_OWN)
                    oi, bank = p3p.next()
                    def mmkm(t, j=j, a=a, b_=b_, bank=bank, buf=buf):
                        return t.matmul(bank[:, 0:b_ - a], KF[buf][:, j * 128:(j + 1) * 128],
                                        Qa[:, a:b_], start=True, stop=True, skip_group_check=True)
                    p.op("tensor", lambda t, f=mmkm: f(t), "PE")
                    if prev_av is not None:
                        emit_prev_av(prev_av)
                        prev_av = None
                    km_mark = p.inc("tensor", "PE")
                    p.wait("scalar", "PE", km_mark)
                    sc_t = p.op("scalar", lambda s, a=a, b_=b_, bank=bank: s.activation(
                        Pbuf[:, a:b_], bank[:, 0:b_ - a], AF.Exp), "SC")
                    p3p.done(oi, "SC", sc_t)
                    if a == q0:
                        p.wait("vector", "SC", sc_t)
                        ve_t = p.op("vector", lambda v, j=j, a=a: v.tensor_tensor(
                            Pbuf[:, a:a + 128], Pbuf[:, a:a + 128],
                            dmask[:, j % 2, :], ALU.mult), "VE")
                        p.drain("vector")
                        prev_av = (j, a, b_, "VE", ve_t)
                    else:
                        prev_av = (j, a, b_, "SC", sc_t)
                    a = b_
            if prev_av is not None:
                emit_prev_av(prev_av)
                prev_av = None
            pe_consumed[h] = p.ctr["PE"]

        if upto <= 3:
            p.barrier()
            p.wait("sync", "ST", p.ctr["ST"])
            p.op("gpsimd", lambda g: g.memset(Pbuf[:], 0.0), "GP")
            p.wait("sync", "GP", p.ctr["GP"])
            for _i in range(S):
                p.dma("sync", out_hbm[_i * 128:(_i + 1) * 128, 0:T_OWN // 2],
                      Pbuf[:].bitcast(F32), "ST")
            p.wait("sync", "ST", p.ctr["ST"])
            with nc.Block() as block:
                p.finalize(block)
            p3ctx.close()
            return nc
        p.barrier()
        p3ctx.close()

        p4actx = ExitStack()
        sb4a = lambda name, shape, dt: mksb(p4actx, name, shape, dt)
        wo2s = sb4a("wo2s", (128, NCH, D), BF16)
        gws = sb4a("gws", (128, NCH, 4), BF16)
        ao_fm = sb4a("ao_fm", (128, NCH, 128), BF16)
        y1q = sb4a("y1q", (128, 2, D), F32)
        x2b = sb4a("x2b", (128, 2, D), F32)
        xfb = sb4a("xfb", (128, D), BF16)
        lgall = sb4a("lgall", (128, 4 * S), F32)
        tmp4a = sb4a("tmp4a", (128, 4 * S), F32)
        m1g = sb4a("m1g", (128, S), F32)
        m2g = sb4a("m2g", (128, S), F32)
        ssum = sb4a("ssum", (128, S), F32)

        xf_fm = unionA[:].rearrange("p (c t) -> p c t", c=NCH)

        n2wtok_sb = sb4a("n2wtok_sb", (128, D), F32)
        load_weight(wo2s, "wo2")
        load_weight(gws, "gw")
        p.dma("gpsimd", n2wtok_sb[:], di["n2wtok"][:, :], "LDW")
        ldw4 = p.ctr["LDW"]
        p.wait("tensor", "LDW", ldw4)
        p.wait("vector", "LDW", ldw4)

        p.wait("sync", "LDG", own_gathers)
        y1_ld = {0: p.dma("sync", y1q[:, 0, :], y1OwnG[0:128, :], lxs(0))}
        x2_st = {}
        x2_sq = {}
        y1_done = {}
        for i in range(S):
            xb2 = i % 2
            if i + 1 < S:
                if i - 1 in y1_done:
                    p.wait("sync", "VE", y1_done[i - 1])
                y1_ld[i + 1] = p.dma("sync", y1q[:, (i + 1) % 2, :],
                                     y1OwnG[(i + 1) * 128:(i + 2) * 128, :], lxs(i + 1))
            if i - 2 in x2_st:
                p.wait("vector", "SX0" if i % 2 == 0 else "SX1", x2_st[i - 2])
                p.wait("vector", "SC", x2_sq[i - 2])
            def eva(v, off, i=i):
                return v.tensor_copy(
                    ao_fm[:, 0:NCH, :],
                    pTR[:, 0:1024].rearrange("p (a b) -> p a b", a=NCH))
            tr_batch([attnout[:, i, jj * 128:(jj + 1) * 128] for jj in range(NCH)], eva)
            p.wait("vector", lxs(i), y1_ld[i])
            for ot, (o0, o1) in enumerate([(256, 768), (768, 1024)]):
                oi, bank = mmp.next()
                def mmo(t, i=i, o0=o0, o1=o1, bank=bank):
                    for c in range(NCH):
                        r = t.matmul(bank[:, 0:o1 - o0], ao_fm[:, c, :],
                                     wo2s[:, c, o0:o1],
                                     start=(c == 0), stop=(c == NCH - 1), skip_group_check=True)
                    return r
                pe_t = p.op("tensor", lambda t, f=mmo: f(t), "PE")
                p.wait("vector", "PE", pe_t)
                ve_t = p.op("vector", lambda v, o0=o0, o1=o1, bank=bank, xb2=xb2, i=i:
                            v.scalar_tensor_tensor(x2b[:, xb2, o0:o1], bank[:, 0:o1 - o0], 1.0,
                                                   y1q[:, i % 2, o0:o1], ALU.mult, ALU.add), "VE")
                mmp.done(oi, "VE", ve_t)
            ve_t = p.op("vector", lambda v, xb2=xb2, i=i: v.tensor_copy(
                x2b[:, xb2, 0:256], y1q[:, i % 2, 0:256]), "VE")
            y1_done[i] = ve_t
            p.wait("sync", "VE", ve_t)
            x2_st[i] = p.dma("sync", x2G[i * 128:(i + 1) * 128, :], x2b[:, xb2, :],
                             "SX0" if i % 2 == 0 else "SX1")
            p.wait("scalar", "VE", ve_t)
            sc_t = p.op("scalar", lambda s, xb2=xb2: s.activation(
                bf_scr[:, 0:D], x2b[:, xb2, :], AF.Square, accum_out=stat["sumsq"][:, 2:3]), "SC")
            x2_sq[i] = sc_t
            p.wait("vector", "SC", sc_t)
            def vxf(v):
                v.tensor_scalar_mul(stat["tmp"][:, 2:3], stat["sumsq"][:, 2:3], 1.0 / D)
                return v.drain()
            ve_t = p.op("vector", lambda v, f=vxf: f(v), "VE")
            p.wait("scalar", "VE", ve_t)
            sc_t = p.op("scalar", lambda s: s.activation(
                stat["tmp2"][:, 2:3], stat["tmp"][:, 2:3], AF.Sqrt, bias=consts[:, 1:2]), "SC")
            p.wait("vector", "SC", sc_t)
            def vxf2(v, xb2=xb2):
                v.reciprocal(stat["r"][:, 2:3], stat["tmp2"][:, 2:3])
                v.drain()
                v.scalar_tensor_tensor(xfb[:], x2b[:, xb2, :], stat["r"][:, 2:3],
                                       n2wtok_sb[:], ALU.mult, ALU.mult)
                return v.drain()
            ve_t = p.op("vector", lambda v, f=vxf2: f(v), "VE")
            p.wait("tensor", "VE", ve_t)
            def evx(v, off, i=i):
                return v.tensor_copy(
                    xf_fm[:, 0:NCH, i * 128:(i + 1) * 128],
                    pTR[:, 0:1024].rearrange("p (a b) -> p a b", a=NCH))
            tr_batch([xfb[:, jj * 128:(jj + 1) * 128] for jj in range(NCH)], evx)
            def mmgate(t, i=i):
                for c in range(NCH):
                    r = t.matmul(pC[:, i * 4:(i + 1) * 4], xf_fm[:, c, i * 128:(i + 1) * 128],
                                 gws[:, c, :], start=(c == 0), stop=(c == NCH - 1),
                                 skip_group_check=True)
                return r
            pe_t = p.op("tensor", lambda t, f=mmgate: f(t), "PE")
            gate_pe_last = pe_t

        lg3 = lgall[:].rearrange("p (s e) -> p s e", s=S)
        lgT = lgall[:].rearrange("p (s e) -> p e s", s=S)
        t4T = tmp4a[:].rearrange("p (s e) -> p e s", s=S)
        t43 = tmp4a[:].rearrange("p (s e) -> p s e", s=S)
        wts3 = wts[:, :, :]
        wtsT = wts_t[:].rearrange("p s e -> p e s")
        m1m = m1g[:].unsqueeze(1).broadcast_to([128, 4, S])
        m2m = m2g[:].unsqueeze(1).broadcast_to([128, 4, S])
        ssm = ssum[:].unsqueeze(1).broadcast_to([128, 4, S])
        p.wait("vector", "PE", gate_pe_last)
        def vtop_a(v):
            v.tensor_copy(lgall[:], pC[:, 0:4 * S])
            v.drain()
            v.tensor_reduce(m1g[:], lg3, AX.X, ALU.max)
            v.drain()
            v.tensor_tensor(lgT, lgT, m1m, ALU.subtract)
            return v.drain()
        ve_t = p.op("vector", lambda v, f=vtop_a: f(v), "VE")
        p.wait("scalar", "VE", ve_t)
        p.wait("tensor", "VE", ve_t)
        sc_t = p.op("scalar", lambda s: s.activation(lgall[:], lgall[:], AF.Exp), "SC")
        p.wait("vector", "SC", sc_t)
        def vtop_b(v):
            v.tensor_reduce(ssum[:], lg3, AX.X, ALU.add)
            v.drain()
            v.reciprocal(ssum[:], ssum[:])
            v.drain()
            v.tensor_tensor(lgT, lgT, ssm, ALU.mult)
            v.drain()
            v.tensor_reduce(m1g[:], lg3, AX.X, ALU.max)
            v.drain()
            v.tensor_tensor(t4T, lgT, m1m, ALU.is_equal)
            v.drain()
            v.scalar_tensor_tensor(tmp4a[:], tmp4a[:], -1e30, lgall[:], ALU.mult, ALU.add)
            v.drain()
            v.tensor_reduce(m2g[:], t43, AX.X, ALU.max)
            v.drain()
            v.tensor_tensor(wtsT, lgT, m2m, ALU.is_ge)
            v.drain()
            v.tensor_tensor(wts3, wts3, lg3, ALU.mult)
            v.drain()
            v.tensor_reduce(ssum[:], wts3, AX.X, ALU.add)
            v.drain()
            v.tensor_scalar(ssum[:], ssum[:], 1e-8, None, ALU.add)
            v.drain()
            v.reciprocal(ssum[:], ssum[:])
            v.drain()
            v.tensor_tensor(wtsT, wtsT, ssm, ALU.mult)
            return v.drain()
        ve_t = p.op("vector", lambda v, f=vtop_b: f(v), "VE")

        if "x2" in dbg:
            p.wait("sync", "ST", p.ctr["ST"])
            p.wait("sync", "SX0", p.ctr["SX0"])
            p.wait("sync", "SX1", p.ctr["SX1"])
            p.wait("sync", "SC", p.ctr["SC"])
            p.wait("sync", "VE", p.ctr["VE"])
            for i in range(S):
                ld = p.dma("sync", x2b[:, 0, :], x2G[i * 128:(i + 1) * 128, :], "LDS")
                p.wait("sync", "LDS", ld)
                p.dma("sync", dbg_t["x2"][i * 128:(i + 1) * 128, :], x2b[:, 0, :], "ST")
                p.wait("sync", "ST", p.ctr["ST"])
        if "wts" in dbg:
            p.wait("sync", "VE", p.ctr["VE"])
            for i in range(S):
                p.dma("sync", dbg_t["wts"][i * 128:(i + 1) * 128, :], wts[:, i, :], "ST")
            p.wait("sync", "ST", p.ctr["ST"])

        p.wait("sync", "SX0", p.ctr["SX0"])
        p.wait("sync", "SX1", p.ctr["SX1"])
        p.barrier()
        p4actx.close()
        attnctx.close()

        p4bctx = ExitStack()
        sb4b = lambda name, shape, dt: mksb(p4bctx, name, shape, dt)
        TH = T_OWN // 2
        NTT = -(-TH // 512)
        h1 = sb4b("h1", (128, HID // 128, TH), BF16)
        h2 = sb4b("h2", (128, HID // 128, TH), BF16)
        moe = sb4b("moe", (128, 6, TH), F32)
        wstr = [sb4b(f"wstr{j}", (128, HID // 128, 128), BF16) for j in range(4)]
        wdn = sb4b("wdn", (128, HID // 128, 768), BF16)
        y1q2 = sb4b("y1q2", (128, D), F32)
        x2rb = sb4b("x2rb", (128, D), F32)
        rbuf = x2rb
        obuf = y1q2

        last_wbuf_pe = {0: None, 1: None, 2: None, 3: None}
        last_wdn_pe = {0: None, 1: None}
        WSEM = ["LW0", "LW1", "LW4", "LW5"]

        def stream_w(name, oc, buf, in_chunks, gate_pe=None):
            sem = WSEM[buf]
            if gate_pe is not None:
                p.wait("gpsimd", "PE", gate_pe)
            src = di[name][:, oc * 128:(oc + 1) * 128].rearrange("(c p) o -> p c o", p=128)
            tgt = p.dma("gpsimd", wstr[buf][:, 0:in_chunks, :], src, sem)
            return (sem, tgt)

        def proj_A(name, in_fm, in_chunks, out_chunks, dst_fm, act, half, dep=None):
            last_ev = None
            for oc in range(out_chunks):
                buf = oc % 4
                sem, tgt = stream_w(name, oc, buf, in_chunks,
                                    gate_pe=last_wbuf_pe[buf])
                p.wait("tensor", sem, tgt)
                if dep is not None:
                    p.wait("tensor", dep[0], dep[1])
                    dep = None
                for tt in range(NTT):
                    ncols = min(512, TH - tt * 512)
                    oi, bank = mmp.next()
                    def mmA(t, oc=oc, buf=buf, tt=tt, ncols=ncols, bank=bank,
                            in_fm=in_fm, in_chunks=in_chunks):
                        for c in range(in_chunks):
                            r = t.matmul(bank[:, 0:ncols], wstr[buf][:, c, :],
                                         in_fm[:, c, tt * 512:tt * 512 + ncols],
                                         start=(c == 0), stop=(c == in_chunks - 1),
                                         skip_group_check=True)
                        return r
                    pe_t = p.op("tensor", lambda t, f=mmA: f(t), "PE")
                    if act is not None:
                        p.wait("scalar", "PE", pe_t)
                        sc_t = p.op("scalar", lambda s, oc=oc, tt=tt, ncols=ncols, bank=bank,
                                    dst_fm=dst_fm, act=act: s.activation(
                                        dst_fm[:, oc, tt * 512:tt * 512 + ncols],
                                        bank[:, 0:ncols], act), "SC")
                        mmp.done(oi, "SC", sc_t)
                        last_ev = ("SC", sc_t)
                    else:
                        p.wait("vector", "PE", pe_t)
                        ve_t = p.op("vector", lambda v, oc=oc, tt=tt, ncols=ncols, bank=bank,
                                    dst_fm=dst_fm: v.tensor_copy(
                                        dst_fm[:, oc, tt * 512:tt * 512 + ncols],
                                        bank[:, 0:ncols]), "VE")
                        mmp.done(oi, "VE", ve_t)
                        last_ev = ("VE", ve_t)
                last_wbuf_pe[buf] = p.ctr["PE"]
            return last_ev

        def proj_B_final(name, in_fm, in_chunks, e_idx, half, dep=None):
            half0 = half * (S // 2)
            tgts = {}
            for ot in range(2):
                o0, o1 = (0, 512) if ot == 0 else (512, 768)
                wsem = "LW2" if ot == 0 else "LW3"
                if last_wdn_pe[ot] is not None:
                    p.wait("gpsimd", "PE", last_wdn_pe[ot])
                tgts[ot] = p.dma("gpsimd", wdn[:, 0:in_chunks, ot * 512:ot * 512 + (o1 - o0)],
                                 di[name][:, 256 + o0:256 + o1].rearrange("(c p) o -> p c o", p=128),
                                 wsem)
            if dep is not None:
                p.wait("tensor", dep[0], dep[1])
            for ot in range(2):
                o0, o1 = (0, 512) if ot == 0 else (512, 768)
                wsem = "LW2" if ot == 0 else "LW3"
                p.wait("tensor", wsem, tgts[ot])
                for sl in range(S // 2):
                    i_glob = half0 + sl
                    oi, bank = mmp.next()
                    def mmB(t, sl=sl, ot=ot, o0=o0, o1=o1, bank=bank, in_fm=in_fm, in_chunks=in_chunks):
                        for c in range(in_chunks):
                            r = t.matmul(bank[:, 0:o1 - o0],
                                         in_fm[:, c, sl * 128:(sl + 1) * 128],
                                         wdn[:, c, ot * 512:ot * 512 + (o1 - o0)],
                                         start=(c == 0), stop=(c == in_chunks - 1),
                                         skip_group_check=True)
                        return r
                    pe_t = p.op("tensor", lambda t, f=mmB: f(t), "PE")
                    p.wait("vector", "PE", pe_t)
                    oc0 = o0 // 128
                    noc = (o1 - o0) // 128
                    if e_idx == 0:
                        def evB(v, sl=sl, oc0=oc0, noc=noc, bank=bank, i_glob=i_glob, e_idx=e_idx):
                            return v.tensor_scalar_mul(
                                moe[:, oc0:oc0 + noc, sl * 128:(sl + 1) * 128],
                                bank[:, 0:noc * 128].rearrange("p (a b) -> p a b", a=noc),
                                wts[:, i_glob, e_idx:e_idx + 1])
                    else:
                        def evB(v, sl=sl, oc0=oc0, noc=noc, bank=bank, i_glob=i_glob, e_idx=e_idx):
                            v.scalar_tensor_tensor(
                                moe[:, oc0:oc0 + noc, sl * 128:(sl + 1) * 128],
                                bank[:, 0:noc * 128].rearrange("p (a b) -> p a b", a=noc),
                                wts[:, i_glob, e_idx:e_idx + 1],
                                moe[:, oc0:oc0 + noc, sl * 128:(sl + 1) * 128],
                                ALU.mult, ALU.add)
                            return v.drain()
                    ve_t = p.op("vector", lambda v, f=evB: f(v), "VE")
                    mmp.done(oi, "VE", ve_t)
                last_wdn_pe[ot] = p.ctr["PE"]

        for half in range(2):
            base = half * TH
            xf_half = xf_fm[:, :, base:base + TH]
            r = proj_A("e0u", xf_half, NCH, HID // 128, h1, AF.Gelu, half)
            proj_B_final("e0d", h1, HID // 128, 0, half, dep=r)
            r1 = proj_A("e1a", xf_half, NCH, HID // 128, h1, AF.Silu, half)
            proj_A("e1b", xf_half, NCH, HID // 128, h2, None, half)
            p.wait("vector", r1[0], r1[1])
            def vmul(v):
                v.drain()
                v.tensor_tensor(h1[:], h1[:], h2[:], ALU.mult)
                return v.drain()
            ve_t = p.op("vector", lambda v, f=vmul: f(v), "VE")
            p.wait("tensor", "VE", ve_t)
            proj_B_final("e1d", h1, HID // 128, 1, half)
            r = proj_A("e2a", xf_half, NCH, HID // 128, h1, AF.Gelu, half)
            r = proj_A("e2b", h1, HID // 128, HID // 128, h2, AF.Gelu, half, dep=r)
            r = proj_A("e2c", h2, HID // 128, HID // 128, h1, AF.Gelu, half, dep=r)
            proj_B_final("e2d", h1, HID // 128, 2, half, dep=r)
            r = proj_A("e3a", xf_half, NCH, NCH, h1, AF.Gelu, half)
            r = proj_A("e3b", h1, NCH, HID // 128, h2, AF.Gelu, half, dep=r)
            proj_B_final("e3c", h2, HID // 128, 3, half, dep=r)

            p.wait("sync", "VE", p.ctr["VE"])
            p.wait("sync", "ST", p.ctr["ST"])
            p.wait("sync", "SX0", p.ctr["SX0"])
            p.wait("sync", "SX1", p.ctr["SX1"])
            for sl in range(S // 2):
                i_glob = half * (S // 2) + sl
                ldx = p.dma("sync", x2rb[:], x2G[i_glob * 128:(i_glob + 1) * 128, :], "LDS")
                p.wait("sync", "LDS", ldx)
                ldy = p.dma("sync", y1q2[:], y1OwnG[i_glob * 128:(i_glob + 1) * 128, :], "LDS")
                p.wait("sync", "LDS", ldy)
                p.wait("vector", "LDS", ldy)
                def vr2(v, sl=sl):
                    v.tensor_tensor(rbuf[:], x2rb[:], y1q2[:], ALU.add)
                    v.drain()
                    v.tensor_tensor(
                        rbuf[:, 256:1024].rearrange("p (a b) -> p a b", a=6),
                        rbuf[:, 256:1024].rearrange("p (a b) -> p a b", a=6),
                        moe[:, 0:6, sl * 128:(sl + 1) * 128], ALU.add)
                    return v.drain()
                ve_t = p.op("vector", lambda v, f=vr2: f(v), "VE")
                stats_ln(rbuf[:], 3, eps_ln_col)
                def vout(v):
                    v.tensor_scalar(obuf[:], rbuf[:], stat["mun"][:, 3:4],
                                    stat["r"][:, 3:4], ALU.add, ALU.mult)
                    return v.drain()
                ve_t = p.op("vector", lambda v, f=vout: f(v), "VE")
                p.wait("sync", "VE", ve_t)
                p.dma("sync", out_hbm[i_glob * 128:(i_glob + 1) * 128, :], obuf[:], "ST")
                p.wait("sync", "ST", p.ctr["ST"])
                p.wait("vector", "ST", p.ctr["ST"])
                p.wait("gpsimd", "ST", p.ctr["ST"])

        p.wait("sync", "ST", p.ctr["ST"])
        p4bctx.close()

        with nc.Block() as block:
            p.finalize(block)

    return nc


N_CHUNKS = 32


def kernel(**inputs):
    nc = build_program(N_CHUNKS)
    in_maps = []
    for b in range(4):
        for parity in range(2):
            in_maps.append(prep_core_inputs(inputs, b, parity, N_CHUNKS))
    res = run_bass_kernel_spmd(nc, in_maps, core_ids=list(range(8)))
    outs = [np.asarray(r["out"], dtype=np.float32) for r in res.results]
    return assemble_output(outs, N_CHUNKS)
